# version 01
# baseline (speedup 1.0000x reference)
import math
from contextlib import ExitStack
import numpy as np
import concourse.bass as bass
import concourse.mybir as mybir
from concourse.bass_utils import run_bass_kernel_spmd

F32 = mybir.dt.float32
BF16 = mybir.dt.bfloat16
AF = mybir.ActivationFunctionType
ALU = mybir.AluOpType
AX = mybir.AxisListType
ts = bass.ts

NCORES = 8
T = 8192
D = 2048
TL = T // NCORES
NTT = TL // 128
NDC = D // 128
EPS = 1e-5

_IDENT = np.eye(128, dtype=np.float32)
_ONES_COL = np.ones((1, 128), dtype=np.float32)


class Q:

    def __init__(self, eng, sem):
        self.e = eng
        self.sem = sem
        self.n = 0

    def op(self, ins, serial=True):
        ins.then_inc(self.sem, 1)
        self.n += 1
        if serial:
            self.e.wait_ge(self.sem, self.n)
        return self.n

    serial_dma = True

    def dma(self, out, in_, **kw):
        self.e.dma_start(out=out, in_=in_, **kw).then_inc(self.sem, 16)
        self.n += 16
        if self.serial_dma:
            self.e.wait_ge(self.sem, self.n)
        return self.n

    def wait(self, other, val):
        if val and val > 0:
            self.e.wait_ge(other.sem if isinstance(other, Q) else other, val)


class Stage:

    def __init__(self, name):
        self.nc = bass.Bass("TRN2", target_bir_lowering=False, debug=False)
        self.name = name
        self.out_names = []

    def inp(self, name, shape, dtype=F32):
        return self.nc.dram_tensor(name, list(shape), dtype, kind="ExternalInput").ap()

    def out(self, name, shape, dtype=F32):
        ap = self.nc.dram_tensor(name, list(shape), dtype, kind="ExternalOutput").ap()
        self.out_names.append(name)
        return ap

    def scratch(self, name, shape, dtype=F32):
        return self.nc.dram_tensor(name, list(shape), dtype, kind="Internal")

    def run(self, in_maps):
        res = run_bass_kernel_spmd(self.nc, in_maps, core_ids=list(range(NCORES)))
        return res.results



def build_norm(st: Stage, mode: str):
    nc = st.nc
    x = st.inp("x", [TL, D])
    ident = st.inp("ident", [128, 128])
    ones_col = st.inp("ones_col", [1, 128])
    w = st.inp("w", [1, D]) if mode == "rms" else None
    xn = st.out("xn", [TL, D])
    xnT = st.out("xnT", [D, TL])

    S = 3 if mode == "rms" else 2
    SM = 4 if mode == "rms" else 0
    VPT = 8 if mode == "ln" else 7
    SPT = 1 + NDC

    with (
        nc.sbuf_tensor("xt", [128, D], F32) as xt,
        nc.sbuf_tensor("xo", [128, D], F32) as xo,
        nc.sbuf_tensor("sq", [128, D], F32) as sq,
        nc.sbuf_tensor("red", [128, 4], F32) as red,
        nc.sbuf_tensor("idt", [128, 128], F32) as idt,
        nc.sbuf_tensor("onc", [1, 128], F32) as onc,
        nc.sbuf_tensor("wB", [128, D], F32) as wB,
        nc.sbuf_tensor("wrow", [1, D], F32) as wrow,
        nc.sbuf_tensor("tev", [128, D], F32) as tev,
        nc.psum_tensor("ptr", [128, 128], F32) as ptr,
        nc.psum_tensor("pwb", [128, 512], F32) as pwb,
        nc.semaphore("q_sy") as q_sy,
        nc.semaphore("q_te") as q_te,
        nc.semaphore("q_sc") as q_sc,
        nc.semaphore("q_ve") as q_ve,
        nc.semaphore("q_gp") as q_gp,
        nc.Block() as block,
    ):
        @block.sync
        def _(sync):
            q = Q(sync, q_sy)
            q.dma(idt[:], ident[:, :])
            q.dma(onc[:], ones_col[:, :])
            if mode == "rms":
                q.dma(wrow[:], w[:, :])
            for i in range(NTT):
                if i > 0:
                    q.wait(q_ve, i * VPT)
                q.dma(xt[:], x[ts(i, 128), :])

        @block.tensor
        def _(tensor):
            q = Q(tensor, q_te)
            if mode == "rms":
                q.wait(q_sy, 16 * S)
                for c in range(4):
                    if c > 0:
                        q.wait(q_sc, c)
                    q.op(tensor.matmul(pwb[:, 0:512], onc[:], wrow[:, ts(c, 512)],
                                       start=True, stop=True), serial=False)
            for i in range(NTT):
                q.wait(q_ve, (i + 1) * VPT)
                for c in range(NDC):
                    g = SM + i * NDC + c
                    if g > 0:
                        if c == 0:
                            q.wait(q_sc, SM + (i - 1) * SPT + 1 + NDC if i > 0 else SM)
                        else:
                            q.wait(q_sc, SM + i * SPT + 1 + c)
                    q.op(tensor.transpose(ptr[:], xo[:, ts(c, 128)], idt[:]),
                         serial=False)

        @block.scalar
        def _(scalar):
            q = Q(scalar, q_sc)
            if mode == "rms":
                for c in range(4):
                    q.wait(q_te, c + 1)
                    q.op(scalar.copy(wB[:, ts(c, 512)], pwb[:, 0:512]))
            for i in range(NTT):
                q.wait(q_ve, i * VPT + (6 if mode == "ln" else 3))
                q.op(scalar.activation(red[:, 3:4], red[:, 2:3], AF.Sqrt))
                for c in range(NDC):
                    q.wait(q_te, SM + i * NDC + c + 1)
                    if i > 0 and c == 0:
                        q.wait(q_gp, 16 * 17 * i)
                    q.op(scalar.copy(tev[:, ts(c, 128)], ptr[:]))

        @block.vector
        def _(vector):
            q = Q(vector, q_ve)
            for i in range(NTT):
                q.wait(q_sy, 16 * (S + i + 1))
                if i > 0:
                    q.wait(q_te, SM + i * NDC)
                    q.wait(q_gp, 16 * (17 * (i - 1) + 1))
                if mode == "ln":
                    q.op(vector.reduce_sum(red[:, 0:1], xt[:], axis=AX.X))
                    q.op(vector.tensor_scalar(red[:, 1:2], red[:, 0:1], 1.0 / D,
                                              None, ALU.mult))
                    q.op(vector.tensor_scalar(xo[:], xt[:], red[:, 1:2], None,
                                              ALU.subtract))
                    q.op(vector.tensor_tensor(sq[:], xo[:], xo[:], ALU.mult))
                    q.op(vector.reduce_sum(red[:, 2:3], sq[:], axis=AX.X))
                    q.op(vector.tensor_scalar(red[:, 2:3], red[:, 2:3], 1.0 / D,
                                              EPS, ALU.mult, ALU.add))
                    q.wait(q_sc, SM + i * SPT + 1)
                    q.op(vector.reciprocal(red[:, 0:1], red[:, 3:4]))
                    q.op(vector.tensor_scalar(xo[:], xo[:], red[:, 0:1], None,
                                              ALU.mult))
                else:
                    if i == 0:
                        q.wait(q_sc, SM)
                    q.op(vector.tensor_tensor(sq[:], xt[:], xt[:], ALU.mult))
                    q.op(vector.reduce_sum(red[:, 2:3], sq[:], axis=AX.X))
                    q.op(vector.tensor_scalar(red[:, 2:3], red[:, 2:3], 1.0 / D,
                                              EPS, ALU.mult, ALU.add))
                    q.wait(q_sc, SM + i * SPT + 1)
                    q.op(vector.reciprocal(red[:, 0:1], red[:, 3:4]))
                    q.op(vector.tensor_scalar(sq[:], xt[:], red[:, 0:1], None,
                                              ALU.mult))
                    q.op(vector.tensor_tensor(xo[:], sq[:], wB[:], ALU.mult))
                    q.op(vector.engine_nop())
            assert q.n == NTT * VPT, (q.n, NTT * VPT)

        @block.gpsimd
        def _(gpsimd):
            q = Q(gpsimd, q_gp)
            for i in range(NTT):
                q.wait(q_ve, (i + 1) * VPT)
                q.dma(xn[ts(i, 128), :], xo[:])
                for c in range(NDC):
                    q.wait(q_sc, SM + i * SPT + 1 + c + 1)
                    q.dma(xnT[ts(c, 128), ts(i, 128)], tev[:, ts(c, 128)])
            gpsimd.wait_ge(q_gp, 16 * 17 * NTT)
    return st


def run_norm(x_slices, mode="ln", w=None):
    st = Stage("norm_" + mode)
    build_norm(st, mode)
    in_maps = []
    for i in range(NCORES):
        m = {"x": np.ascontiguousarray(x_slices[i]), "ident": _IDENT,
             "ones_col": _ONES_COL}
        if mode == "rms":
            m["w"] = np.ascontiguousarray(np.asarray(w, np.float32).reshape(1, D))
        in_maps.append(m)
    res = st.run(in_maps)
    xn = [res[i]["xn"] for i in range(NCORES)]
    xnT = [res[i]["xnT"] for i in range(NCORES)]
    return xn, xnT


def kernel(**inputs):
    raise NotImplementedError("under construction")



def build_l0ffn(st: Stage):
    nc = st.nc
    x = st.inp("x", [TL, D])
    xnT = st.inp("xnT", [D, TL])
    atc = st.inp("ATcat", [D, 512])
    bwt = st.inp("BwT", [8, 64, D])
    wgt = st.inp("WgT", [D, 8])
    ident = st.inp("ident", [128, 128])
    xo_out = st.out("xo_new", [TL, D])

    SL = 4
    TPT = 72
    SPT = 17
    VPT = 12

    with ExitStack() as ctx:
        sb = lambda n, s, d=F32: ctx.enter_context(nc.sbuf_tensor(n, s, d))
        ps = lambda n, s, d=F32: ctx.enter_context(nc.psum_tensor(n, s, d))
        sem = lambda n: ctx.enter_context(nc.semaphore(n))
        xnt_sb = sb("xnt_sb", [128, NDC * 128])
        x_sb = sb("x_sb", [128, D])
        at_sb = sb("at_sb", [128, NDC * 512])
        bw_sb = sb("bw_sb", [64, 8 * D])
        wg_sb = sb("wg_sb", [128, NDC * 8])
        idt = sb("idt", [128, 128])
        lg = sb("lg", [128, 8])
        pr = sb("pr", [128, 8])
        red = sb("red", [128, 4])
        mask = sb("mask", [128, 8])
        h_sb = sb("h_sb", [128, 512])
        ht_sb = sb("ht_sb", [64, 8 * 128])
        xnew = sb("xnew", [128, D])
        ps_r = ps("ps_r", [128, 8])
        ps_h = ps("ps_h", [128, 512])
        ps_t = ps("ps_t", [64, 128])
        ps_y = ps("ps_y", [128, D])
        q_sy = sem("q_sy"); q_te = sem("q_te"); q_sc = sem("q_sc")
        q_ve = sem("q_ve"); q_gp = sem("q_gp")
        block = ctx.enter_context(nc.Block())
        @block.sync
        def _(sync):
            q = Q(sync, q_sy)
            q.dma(at_sb[:].rearrange("p (c n) -> p c n", c=NDC),
                  atc.rearrange("(c p) n -> p c n", p=128))
            q.dma(bw_sb[:].rearrange("p (e n) -> p e n", e=8),
                  bwt.rearrange("e p n -> p e n"))
            q.dma(wg_sb[:].rearrange("p (c n) -> p c n", c=NDC),
                  wgt.rearrange("(c p) n -> p c n", p=128))
            q.dma(idt[:], ident[:, :])
            for i in range(NTT):
                if i > 0:
                    q.wait(q_te, (i - 1) * TPT + 32)
                    q.wait(q_ve, i * VPT)
                q.dma(xnt_sb[:].rearrange("p (c t) -> p c t", c=NDC),
                      xnT[:, ts(i, 128)].rearrange("(c p) t -> p c t", p=128))
                q.dma(x_sb[:], x[ts(i, 128), :])

        @block.tensor
        def _(tensor):
            q = Q(tensor, q_te)
            q.wait(q_sy, 16 * SL)
            for i in range(NTT):
                base_t = i * TPT
                q.wait(q_sy, 16 * (SL + 2 * i + 1))
                if i > 0:
                    q.wait(q_ve, (i - 1) * VPT + 2)
                for c in range(NDC):
                    q.op(tensor.matmul(ps_r[:], xnt_sb[:, ts(c, 128)],
                                       wg_sb[:, ts(c, 8)], start=(c == 0),
                                       stop=(c == NDC - 1)), serial=False)
                if i > 0:
                    q.wait(q_sc, (i - 1) * SPT + 9)
                for c in range(NDC):
                    q.op(tensor.matmul(ps_h[:], xnt_sb[:, ts(c, 128)],
                                       at_sb[:, ts(c, 512)], start=(c == 0),
                                       stop=(c == NDC - 1)), serial=False)
                for e in range(8):
                    q.wait(q_sc, i * SPT + 1 + e + 1)
                    if e > 0:
                        q.wait(q_sc, i * SPT + 9 + e)
                    elif i > 0:
                        q.wait(q_sc, i * SPT)
                    q.op(tensor.transpose(ps_t[:], h_sb[:, ts(e, 64)], idt[:]),
                         serial=False)
                if i > 0:
                    q.wait(q_ve, i * VPT)
                for e in range(8):
                    q.wait(q_sc, i * SPT + 9 + e + 1)
                    for k in range(4):
                        q.op(tensor.matmul(ps_y[:, ts(k, 512)], ht_sb[:, ts(e, 128)],
                                           bw_sb[:, e * D + k * 512: e * D + (k + 1) * 512],
                                           start=(e == 0), stop=(e == 7),
                                           skip_group_check=True), serial=False)
            assert q.n == NTT * TPT, (q.n,)

        @block.scalar
        def _(scalar):
            q = Q(scalar, q_sc)
            for i in range(NTT):
                q.wait(q_ve, i * VPT + 2)
                q.op(scalar.activation(pr[:], lg[:], AF.Exp))
                q.wait(q_ve, i * VPT + 7)
                q.wait(q_te, i * TPT + 32)
                for e in range(8):
                    q.op(scalar.activation(h_sb[:, ts(e, 64)], ps_h[:, ts(e, 64)],
                                           AF.Copy, scale=mask[:, e:e + 1]))
                for e in range(8):
                    q.wait(q_te, i * TPT + 32 + e + 1)
                    q.op(scalar.copy(ht_sb[:, ts(e, 128)], ps_t[:]))
            assert q.n == NTT * SPT, (q.n,)

        @block.vector
        def _(vector):
            q = Q(vector, q_ve)
            for i in range(NTT):
                q.wait(q_te, i * TPT + 16)
                q.op(vector.reduce_max(red[:, 0:1], ps_r[:], axis=AX.X,
                                       negate=True))
                if i > 0:
                    q.wait(q_sc, (i - 1) * SPT + 1)
                q.op(vector.tensor_scalar(lg[:], ps_r[:], red[:, 0:1], None,
                                          ALU.add))
                q.wait(q_sc, i * SPT + 1)
                q.op(vector.reduce_sum(red[:, 1:2], pr[:], axis=AX.X))
                q.op(vector.reciprocal(red[:, 2:3], red[:, 1:2]))
                q.op(vector.tensor_scalar(pr[:], pr[:], red[:, 2:3], None,
                                          ALU.mult))
                q.op(vector.reduce_max(red[:, 3:4], pr[:], axis=AX.X))
                q.op(vector.tensor_single_scalar(mask[:], pr[:], red[:, 3:4],
                                                 ALU.is_equal))
                q.op(vector.tensor_scalar(red[:, 0:1], red[:, 3:4], 2.0, None,
                                          ALU.mult))
                q.wait(q_te, (i + 1) * TPT)
                q.wait(q_sy, 16 * (SL + 2 * i + 2))
                if i > 0:
                    q.wait(q_gp, 16 * i)
                for k in range(4):
                    q.op(vector.scalar_tensor_tensor(xnew[:, ts(k, 512)],
                                                     ps_y[:, ts(k, 512)],
                                                     red[:, 0:1],
                                                     x_sb[:, ts(k, 512)],
                                                     ALU.mult, ALU.add))
            assert q.n == NTT * VPT, (q.n,)

        @block.gpsimd
        def _(gpsimd):
            q = Q(gpsimd, q_gp)
            for i in range(NTT):
                q.wait(q_ve, (i + 1) * VPT)
                q.dma(xo_out[ts(i, 128), :], xnew[:])
            gpsimd.wait_ge(q_gp, 16 * NTT)
    return st


def run_l0ffn(x_slices, xnT_slices, A, Bw, Wg):
    st = Stage("l0ffn")
    build_l0ffn(st)
    ATcat = np.ascontiguousarray(A.transpose(2, 0, 1).reshape(D, 512))
    BwT = np.ascontiguousarray(Bw.transpose(0, 2, 1))
    WgT = np.ascontiguousarray(Wg.T)
    in_maps = []
    for i in range(NCORES):
        in_maps.append({"x": np.ascontiguousarray(x_slices[i]),
                        "xnT": np.ascontiguousarray(xnT_slices[i]),
                        "ATcat": ATcat, "BwT": BwT, "WgT": WgT, "ident": _IDENT})
    res = st.run(in_maps)
    return [res[i]["xo_new"] for i in range(NCORES)]



class Milestones(dict):
    def __missing__(self, key):
        return 0

    def set(self, key, val):
        if key in self and self[key] != val:
            raise AssertionError(f"milestone {key}: {self[key]} != {val}")
        self[key] = val


def twopass(build_emit, st: Stage):
    M = Milestones()
    dry = Stage(st.name + "_dry")
    build_emit(dry, M)
    build_emit(st, M)
    return st



NJT = T // 128
NQG = 8
CLAMP = 32.0
NTC = T // 512
ATT_SCALE = 1.0 / math.sqrt(128.0)


def emit_l0attn(st: Stage, M: Milestones):
    nc = st.nc
    x = st.inp("x", [TL, D])
    xnT = st.inp("xnT", [D, TL])
    wqT = st.inp("WqT", [D, 256])
    wkT = st.inp("WkT", [D, 256])
    wvT = st.inp("WvT", [D, 256])
    wgTi = st.inp("WgT", [D, 256])
    ws1T = st.inp("Ws1T", [D, 16])
    ws2T = st.inp("Ws2T", [16, 128])
    bs2 = st.inp("bs2", [128, 1])
    gnw = st.inp("gnw", [1, 128])
    woT = st.inp("WoT", [256, D])
    cosTM = st.inp("cosTM", [T, 64])
    sinTM = st.inp("sinTM", [T, 64])
    ident = st.inp("ident", [128, 128])
    ones_col = st.inp("ones_col", [1, 128])
    x_new = st.out("x_new", [TL, D])

    ag_in = st.scratch("ag_in", [D, TL])
    ag_out = st.scratch("ag_out", [NCORES * D, TL])
    d_qT = st.scratch("d_qT", [256, T])
    d_kT = st.scratch("d_kT", [256, T])
    d_v = st.scratch("d_v", [T, 256])
    d_g = st.scratch("d_g", [T, 256])
    d_esT = st.scratch("d_esT", [128, T])
    d_es = st.scratch("d_es", [T, 128])
    d_enz = st.scratch("d_enz", [T, 128])
    d_outp = st.scratch("d_outp", [T, D])
    d_rs = st.scratch("d_rs", [TL, D])

    cpr = max(1, TL // 512)
    def hs_chunk(c, n):
        r, tl = n // cpr, (n % cpr) * 512
        return ag_out.ap()[r * D + c * 128: r * D + (c + 1) * 128, tl: tl + 512]

    with ExitStack() as ctx:
        sb = lambda nm, s, d=F32: ctx.enter_context(nc.sbuf_tensor(nm, s, d))
        sem = lambda nm: ctx.enter_context(nc.semaphore(nm))
        hsb = [sb(f"hsb{k}", [128, 512]) for k in range(2)]
        arena = sb("arena", [128, 18432])
        w_q = arena[:, 0:4096]
        w_k = arena[:, 4096:8192]
        w_v = arena[:, 8192:12288]
        w_g = arena[:, 12288:16384]
        cosb = [arena[:, 16384 + k * 256: 16384 + (k + 1) * 256]
                for k in range(2)]
        sinb = [arena[:, 16896 + k * 256: 16896 + (k + 1) * 256]
                for k in range(2)]
        qtg = arena[:, 0:1024]
        gg = arena[:, 1024:2048]
        atg = arena[:, 2048:3072]
        qvet = arena[0:64, 3072:4096]
        ocp = arena[:, 4096:5120]
        outp_sb = arena[:, 5120:7168]
        enzg = arena[:, 7168:7680]
        xr = arena[:, 7680:9728]
        rr = arena[:, 9728:11776]
        okf = arena[:, 11776:11840]
        pexp = arena[:, 11840:11904]
        silg = arena[:, 11904:12032]
        ogv = arena[:, 12032:12160]
        ogt = arena[:, 12160:12288]
        sqv = arena[:, 12288:12416]
        ones_t = arena[:, 0:T]
        w_s1 = sb("w_s1", [128, NDC * 16])
        w_s2 = sb("w_s2", [16, 128])
        bs2_sb = sb("bs2_sb", [128, 1])
        w_o = sb("w_o", [128, 2 * D])
        idt = sb("idt", [128, 128])
        onc = sb("onc", [1, 128])
        gnw_row = sb("gnw_row", [1, 128])
        gnwB = sb("gnwB", [128, 128])
        qraw = [sb(f"qraw{k}", [128, 256]) for k in range(2)]
        qout = [sb(f"qout{k}", [128, 256]) for k in range(2)]
        vg_ev = [sb(f"vg_ev{k}", [128, 256]) for k in range(2)]
        s1_sb = sb("s1_sb", [16, 512])
        rsc = sb("rsc", [128, 512])
        esT_sb = sb("esT_sb", [128, T])
        cs_sb = sb("cs_sb", [128, T])
        enzT_sb = cs_sb
        tr_ev = [sb(f"tr_ev{k}", [128, 128]) for k in range(4)]
        kj = [sb(f"kj{k}", [128, 128]) for k in range(2)]
        esj = [sb(f"esj{k}", [128, 64]) for k in range(2)]
        estj = [sb(f"estj{k}", [64, 128]) for k in range(2)]
        vj = [sb(f"vj{k}", [128, 128]) for k in range(2)]
        red = sb("red", [128, 32])
        ps_all = ctx.enter_context(nc.psum_tensor("ps_all", [128, 4096], F32))
        q_sy = sem("q_sy"); q_te = sem("q_te"); q_sc = sem("q_sc")
        q_ve = sem("q_ve"); q_gp = sem("q_gp"); q_cc = sem("q_cc")
        q_gq = sem("q_gq")
        block = ctx.enter_context(nc.Block())

        pr_qt = [ps_all[:, k * 256: (k + 1) * 256] for k in range(4)]
        pr_kt = [ps_all[:, 1024 + k * 256: 1024 + (k + 1) * 256] for k in range(4)]
        pr_v = [ps_all[:, 2048 + k * 256: 2048 + (k + 1) * 256] for k in range(4)]
        pr_g = [ps_all[:, 3072 + k * 256: 3072 + (k + 1) * 256] for k in range(4)]
        pr_trm = ps_all[:, 0:128]
        pr_s1 = ps_all[0:16, 2048:2560]
        pr_s2 = ps_all[:, 0:512]
        pr_at = [ps_all[:, 0:512], ps_all[:, 512:1024]]
        pr_ok = ps_all[:, 1024:1536]
        pr_o = ps_all[:, 1536:2560]
        pr_tr = ps_all[:, 2560:2688]
        pr_out = ps_all[:, 3072:4096]

        def qts_of(g):
            return list(range(g * 8, g * 8 + 8))

        @block.sync
        def _(sync):
            q = Q(sync, q_sy)
            q.dma(w_q.rearrange("p (c n) -> p c n", c=NDC),
                  wqT.rearrange("(c p) n -> p c n", p=128))
            q.dma(w_k.rearrange("p (c n) -> p c n", c=NDC),
                  wkT.rearrange("(c p) n -> p c n", p=128))
            q.dma(w_v.rearrange("p (c n) -> p c n", c=NDC),
                  wvT.rearrange("(c p) n -> p c n", p=128))
            q.dma(w_g.rearrange("p (c n) -> p c n", c=NDC),
                  wgTi.rearrange("(c p) n -> p c n", p=128))
            q.dma(w_s1[:].rearrange("p (c n) -> p c n", c=NDC),
                  ws1T.rearrange("(c p) n -> p c n", p=128))
            q.dma(w_s2[:], ws2T[:, :])
            q.dma(bs2_sb[:], bs2[:, :])
            q.dma(w_o[:].rearrange("p (h n) -> p h n", h=2),
                  woT.rearrange("(h p) n -> p h n", p=128))
            q.dma(idt[:], ident[:, :])
            q.dma(onc[:], ones_col[:, :])
            q.dma(gnw_row[:], gnw[:, :])
            M.set("wload", q.n)
            q.wait(q_cc, M["ag_done"])
            for n in range(NTC):
                if n >= 2:
                    q.wait(q_ve, M[f"rope_{n - 2}_7"])
                q.dma(cosb[n % 2].rearrange("p (a m) -> p a m", a=4),
                      cosTM[n * 512:(n + 1) * 512, :]
                      .rearrange("(a p) m -> p a m", p=128))
                q.dma(sinb[n % 2].rearrange("p (a m) -> p a m", a=4),
                      sinTM[n * 512:(n + 1) * 512, :]
                      .rearrange("(a p) m -> p a m", p=128))
                M.set(f"csld_{n}", q.n)
                for c in range(NDC):
                    if n + c > 0:
                        q.wait(q_te, M[f"Bmmprev_{n}_{c}"])
                    q.dma(hsb[(n * NDC + c) % 2][:], hs_chunk(c, n))
                    M.set(f"Bld_{n}_{c}", q.n)
            for n in range(NTC):
                for c in range(NDC):
                    q.wait(q_te, M[f"Smmprev_{n}_{c}"])
                    q.dma(hsb[(n * NDC + c) % 2][:], hs_chunk(c, n))
                    M.set(f"Sld_{n}_{c}", q.n)
            for h in range(2):
                for g in range(NQG):
                    jmax = g * 8 + 8
                    q.wait(q_te, M[f"Cgrp_free_{h}_{g}"])
                    q.wait(q_ve, M[f"Cgrp_free_ve_{h}_{g}"])
                    q.wait(q_sc, M[f"Cgrp_free_sc_{h}_{g}"])
                    if h == 0 and g == 0:
                        q.wait(q_ve, M["enz_done"])
                        q.wait(q_gp, M[f"trst_1_{NJT - 1}"])
                    q.dma(qtg, d_qT.ap()[h * 128:(h + 1) * 128,
                                            g * 1024:(g + 1) * 1024])
                    q.dma(enzg.rearrange("p (a m) -> p a m", a=8),
                          d_enz.ap()[g * 1024:(g + 1) * 1024,
                                     h * 64:(h + 1) * 64]
                          .rearrange("(a p) m -> p a m", p=128))
                    q.dma(gg.rearrange("p (a m) -> p a m", a=8),
                          d_g.ap()[g * 1024:(g + 1) * 1024,
                                   h * 128:(h + 1) * 128]
                          .rearrange("(a p) m -> p a m", p=128))
                    M.set(f"Cg_{h}_{g}", q.n)
                    for j in range(jmax):
                        if j >= 2:
                            q.wait(q_te, M[f"okmm_{h}_{g}_{j - 2}"])
                        q.dma(kj[j % 2][:], d_kT.ap()[h * 128:(h + 1) * 128,
                                                      j * 128:(j + 1) * 128])
                        q.dma(esj[j % 2][:], d_es.ap()[j * 128:(j + 1) * 128,
                                                       h * 64:(h + 1) * 64])
                        M.set(f"kld_{h}_{g}_{j}", q.n)
                    for j in range(jmax):
                        if j >= 2:
                            q.wait(q_te, M[f"omm_{h}_{g}_{j - 2}"])
                        q.dma(estj[j % 2][:], d_esT.ap()[h * 64:(h + 1) * 64,
                                                         j * 128:(j + 1) * 128])
                        q.dma(vj[j % 2][:], d_v.ap()[j * 128:(j + 1) * 128,
                                                     h * 128:(h + 1) * 128])
                        M.set(f"vld_{h}_{g}_{j}", q.n)
            for i in range(NTT):
                q.wait(q_cc, M["rs_done"])
                if i > 0:
                    q.wait(q_ve, M[f"resadd_{i - 1}"])
                    q.wait(q_gp, M[f"xst_{i - 1}"])
                q.dma(xr, x[ts(i, 128), :])
                q.dma(rr, d_rs.ap()[ts(i, 128), :])
                M.set(f"Dld_{i}", q.n)

        @block.tensor
        def _(tensor):
            q = Q(tensor, q_te)
            q.wait(q_sy, M["wload"])
            q.wait(q_ve, M["wq_scaled"])
            q.op(tensor.matmul(pr_tr[:, 0:128], onc[:], gnw_row[:],
                               start=True, stop=True), serial=False)
            M.set("gnwB_mm", q.n)
            last_ptev = None
            for n in range(NTC):
                if n > 0:
                    q.wait(q_sc, M[f"evB_{n - 1}"])
                else:
                    q.wait(q_sc, M["gnwB_ev"])
                for c in range(NDC):
                    M.set(f"Bmmprev_{n}_{c}", q.n - 16 if (n + c) else 0)
                    q.wait(q_sy, M[f"Bld_{n}_{c}"])
                    sp_ = (c == NDC - 1)
                    b = hsb[(n * NDC + c) % 2]
                    for prs, wsb in ((pr_qt, w_q), (pr_kt, w_k),
                                     (pr_v, w_v), (pr_g, w_g)):
                        for tt in range(4):
                            q.op(tensor.matmul(
                                prs[tt], b[:, ts(tt, 128)], wsb[:, ts(c, 256)],
                                start=(c == 0 and tt % 2 == 0), stop=sp_,
                                skip_group_check=True), serial=False)
                M.set(f"Bmm_{n}", q.n)
                for u in range(8):
                    for h in range(2):
                        q.wait(q_ve, M[f"rope_{n}_{u}"])
                        q.wait(q_sc, M[f"pev_{n}_1"])
                        if last_ptev is not None:
                            q.wait(q_sc, M[last_ptev])
                        q.op(tensor.transpose(pr_trm,
                                              qout[u % 2][:, h * 128:(h + 1) * 128],
                                              idt[:]), serial=False)
                        M.set(f"ptT_{n}_{u}_{h}", q.n)
                        last_ptev = f"ptev_{n}_{u}_{h}" 
            q.wait(q_sc, M[f"evB_{NTC - 1}"])
            for n in range(NTC):
                for c in range(NDC):
                    M.set(f"Smmprev_{n}_{c}",
                          q.n - 1 if (n + c) else M[f"Bmm_{NTC - 1}"])
                    q.wait(q_sy, M[f"Sld_{n}_{c}"])
                    b = hsb[(n * NDC + c) % 2]
                    q.op(tensor.matmul(pr_s1, w_s1[:, ts(c, 16)], b[:],
                                       start=(c == 0), stop=(c == NDC - 1)),
                         serial=False)
                M.set(f"Smm_{n}", q.n)
                q.wait(q_sc, M[f"s1ev_{n}"])
                if n > 0:
                    q.wait(q_sc, M[f"esx_{n - 1}"])
                q.op(tensor.matmul(pr_s2, w_s2[:], s1_sb[:], start=True,
                                   stop=True), serial=False)
                M.set(f"s2mm_{n}", q.n)
            for z, (src, dst) in enumerate([(esT_sb, d_es), (enzT_sb, d_enz)]):
                for jt in range(NJT):
                    if z + jt > 0:
                        q.wait(q_sc, M[f"trev_{z}_{jt}_prev"])
                    if z == 0:
                        q.wait(q_ve, M["scan_done"])
                    else:
                        q.wait(q_ve, M["enz_done"])
                    q.op(tensor.transpose(pr_tr[:, 0:128], src[:, ts(jt, 128)],
                                          idt[:]), serial=False)
                    M.set(f"trT_{z}_{jt}", q.n)
            for h in range(2):
                for g in range(NQG):
                    jmax = g * 8 + 8
                    qts = qts_of(g)
                    M.set(f"Cgrp_free_{h}_{g}", M.get(f"Cgrp_last_te_{h}_{g}", 0))
                    q.wait(q_sy, M[f"Cg_{h}_{g}"])
                    for j in range(jmax):
                        q.wait(q_sy, M[f"kld_{h}_{g}_{j}"])
                        if j > 0:
                            q.wait(q_sc, M[f"atev_{h}_{g}_{j - 1}"])
                        elif M[f"prev_okmm_done_{h}_{g}"] >= 0:
                            q.wait(q_sc, M[f"prev_atev_{h}_{g}"])
                        for half in range(2):
                            q.op(tensor.matmul(pr_at[half], kj[j % 2][:],
                                               qtg[:, half * 512:(half + 1) * 512],
                                               start=True, stop=True), serial=False)
                        M.set(f"atmm_{h}_{g}_{j}", q.n)
                        q.wait(q_sc, M[f"atev_{h}_{g}_{j}"])
                        if j in qts:
                            q.wait(q_gq, M[f"atsel_{h}_{g}_{j}"])
                        if j == 0:
                            q.wait(q_ve, M[f"Cgrp_free_ve_{h}_{g}"])
                        first = (j == 0)
                        for qt in qts:
                            if qt < j:
                                continue
                            qtl = qt - g * 8
                            q.op(tensor.matmul(
                                pr_ok[:, qtl * 64:(qtl + 1) * 64],
                                atg[:, qtl * 128:(qtl + 1) * 128],
                                esj[j % 2][:],
                                start=(first and qtl == 0), stop=False,
                                skip_group_check=True), serial=False)
                        M.set(f"okmm_{h}_{g}_{j}", q.n)
                    for qtl in range(8):
                        q.wait(q_ve, M[f"qve_{h}_{g}_{qtl}"])
                        if qtl > 0:
                            q.wait(q_sc, M[f"qvetev_{h}_{g}_{qtl - 1}"])
                        else:
                            q.wait(q_sc, M[f"trprev_{h}_{g}"])
                        q.op(tensor.transpose(pr_tr[0:64, 0:128], pexp, idt[:]),
                             serial=False)
                        M.set(f"qvT_{h}_{g}_{qtl}", q.n)
                    for j in range(jmax):
                        q.wait(q_sy, M[f"vld_{h}_{g}_{j}"])
                        q.wait(q_sc, M[f"qvetev_{h}_{g}_7"])
                        if j > 0:
                            q.wait(q_sc, M[f"w2ev_{h}_{g}_{j - 1}"])
                        for half in range(2):
                            q.op(tensor.matmul(pr_at[half], estj[j % 2][:],
                                               qvet[:, half * 512:(half + 1) * 512],
                                               start=True, stop=True), serial=False)
                        M.set(f"w2mm_{h}_{g}_{j}", q.n)
                        q.wait(q_sc, M[f"w2ev_{h}_{g}_{j}"])
                        if j in qts:
                            q.wait(q_gq, M[f"w2sel_{h}_{g}_{j}"])
                        first = (j == 0)
                        for qt in qts:
                            if qt < j:
                                continue
                            qtl = qt - g * 8
                            q.op(tensor.matmul(
                                pr_o[:, qtl * 128:(qtl + 1) * 128],
                                atg[:, qtl * 128:(qtl + 1) * 128],
                                vj[j % 2][:],
                                start=(first and qtl % 4 == 0), stop=False,
                                skip_group_check=True), serial=False)
                        M.set(f"omm_{h}_{g}_{j}", q.n)
                    for qtl in range(8):
                        q.wait(q_ve, M[f"ogv_{h}_{g}_{qtl}"])
                        q.wait(q_sc, M[f"ogtev_prev_{h}_{g}_{qtl}"])
                        q.op(tensor.transpose(pr_tr[:, 0:128], ogv, idt[:]),
                             serial=False)
                        M.set(f"ogT_{h}_{g}_{qtl}", q.n)
                        q.wait(q_sc, M[f"ogtev_{h}_{g}_{qtl}"])
                        if True:
                            q.wait(q_ve, M[f"outev_prevA_{h}_{g}_{qtl}"])
                            for kk in range(2):
                                q.op(tensor.matmul(
                                    pr_out[:, kk * 512:(kk + 1) * 512], ogt,
                                    w_o[:, h * D + kk * 512: h * D + (kk + 1) * 512],
                                    start=True, stop=True), serial=False)
                            M.set(f"outmmA_{h}_{g}_{qtl}", q.n)
                            q.wait(q_ve, M[f"outevA_{h}_{g}_{qtl}"])
                            for kk in range(2):
                                q.op(tensor.matmul(
                                    pr_out[:, kk * 512:(kk + 1) * 512], ogt,
                                    w_o[:, h * D + 1024 + kk * 512:
                                        h * D + 1024 + (kk + 1) * 512],
                                    start=True, stop=True), serial=False)
                            M.set(f"outmmB_{h}_{g}_{qtl}", q.n)
                    M.set(f"Cgrp_last_te_{h + (g + 1) // NQG}_{(g + 1) % NQG}", q.n)
            M.set("te_end", q.n)

        @block.scalar
        def _(scalar):
            q = Q(scalar, q_sc)
            q.wait(q_te, M["gnwB_mm"])
            q.op(scalar.copy(gnwB[:], pr_tr[:, 0:128]))
            M.set("gnwB_ev", q.n)
            for n in range(NTC):
                q.wait(q_te, M[f"Bmm_{n}"])

                def _ptev(uu):
                    for h in range(2):
                        q.wait(q_te, M[f"ptT_{n}_{uu}_{h}"])
                        gi = n * 16 + uu * 2 + h
                        if gi >= 4:
                            pz = gi - 4
                            q.wait(q_gp, M[f"qkst_{pz // 16}_{(pz % 16) // 2}_"
                                           f"{pz % 2}"])
                        q.op(scalar.copy(tr_ev[gi % 4][:], pr_trm))
                        M.set(f"ptev_{n}_{uu}_{h}", q.n)

                for u in range(8):
                    if n * 8 + u >= 2:
                        pz = n * 8 + u - 2
                        q.wait(q_ve, M[f"rope_{pz // 8}_{pz % 8}"])
                    prt = pr_qt[u] if u < 4 else pr_kt[u - 4]
                    q.op(scalar.copy(qraw[u % 2][:], prt))
                    M.set(f"pev_{n}_{u}", q.n)
                    if u >= 2:
                        _ptev(u - 2)
                _ptev(6)
                _ptev(7)
                for u in range(8):
                    if n * 8 + u >= 2:
                        q.wait(q_gp, M[f"vgst_{(n * 8 + u - 2) // 8}_"
                                       f"{(n * 8 + u - 2) % 8}"])
                    prt = pr_v[u] if u < 4 else pr_g[u - 4]
                    q.op(scalar.copy(vg_ev[u % 2][:], prt))
                    M.set(f"vgev_{n}_{u}", q.n)
                M.set(f"evB_{n}", q.n)
            for n in range(NTC):
                q.wait(q_te, M[f"Smm_{n}"])
                q.op(scalar.copy(s1_sb[:], pr_s1))
                M.set(f"s1ev_{n}", q.n)
                q.wait(q_ve, M[f"esv_{n}"])
                q.op(scalar.activation(esT_sb[:, ts(n, 512)], rsc[:], AF.Exp))
                M.set(f"esx_{n}", q.n)
            for z in range(2):
                for jt in range(NJT):
                    M.set(f"trev_{z}_{jt}_prev", q.n)
                    q.wait(q_te, M[f"trT_{z}_{jt}"])
                    if z * NJT + jt >= 2:
                        pz = z * NJT + jt - 2
                        q.wait(q_gp, M[f"trst_{pz // NJT}_{pz % NJT}"])
                    q.op(scalar.copy(tr_ev[jt % 2][:], pr_tr[:, 0:128]))
                    M.set(f"trev_{z}_{jt}", q.n)
            for h in range(2):
                for g in range(NQG):
                    jmax = g * 8 + 8
                    qts = qts_of(g)
                    M.set(f"Cgrp_free_sc_{h}_{g}", q.n)
                    M.set(f"prev_atev_{h}_{g}", q.n)
                    for j in range(jmax):
                        q.wait(q_te, M[f"atmm_{h}_{g}_{j}"])
                        if j == 0:
                            q.wait(q_te, M[f"prev_ok_te_{h}_{g}"])
                        for half in range(2):
                            q.op(scalar.copy(atg[:, half * 512:(half + 1) * 512],
                                             pr_at[half]))
                        M.set(f"atev_{h}_{g}_{j}", q.n)
                    for qtl in range(8):
                        q.wait(q_ve, M[f"okv_{h}_{g}_{qtl}"])
                        if qtl > 0:
                            q.wait(q_te, M[f"qvT_{h}_{g}_{qtl - 1}"])
                        q.op(scalar.activation(pexp, okf, AF.Exp,
                                               bias=red[:, 0:1]))
                        M.set(f"exp_{h}_{g}_{qtl}", q.n)
                        M.set(f"trprev_{h}_{g}", q.n) if qtl == 0 else None
                        q.wait(q_te, M[f"qvT_{h}_{g}_{qtl}"])
                        q.op(scalar.copy(qvet[:, ts(qtl, 128)], pr_tr[0:64, 0:128]))
                        M.set(f"qvetev_{h}_{g}_{qtl}", q.n)
                    for j in range(jmax):
                        q.wait(q_te, M[f"w2mm_{h}_{g}_{j}"])
                        if j == 0:
                            q.wait(q_te, M[f"okmm_{h}_{g}_{jmax - 1}"])
                        for half in range(2):
                            q.op(scalar.copy(atg[:, half * 512:(half + 1) * 512],
                                             pr_at[half]))
                        M.set(f"w2ev_{h}_{g}_{j}", q.n)
                    for qtl in range(8):
                        q.wait(q_ve, M[f"msv_{h}_{g}_{qtl}"])
                        q.op(scalar.activation(red[:, 12 + qtl:13 + qtl],
                                               red[:, 4 + qtl:5 + qtl], AF.Ln))
                        q.op(scalar.activation(red[:, 20 + qtl:21 + qtl],
                                               red[:, 12 + qtl:13 + qtl], AF.Exp,
                                               scale=-0.5))
                        M.set(f"rstd_{h}_{g}_{qtl}", q.n)
                    for qtl in range(8):
                        if qtl > 0:
                            q.wait(q_ve, M[f"ogv_{h}_{g}_{qtl - 1}"])
                        q.op(scalar.activation(silg, gg[:, ts(qtl, 128)],
                                               AF.Silu))
                        M.set(f"sil_{h}_{g}_{qtl}", q.n)
                        M.set(f"ogtev_prev_{h}_{g}_{qtl}", q.n)
                        q.wait(q_te, M[f"ogT_{h}_{g}_{qtl}"])
                        q.op(scalar.copy(ogt, pr_tr[:, 0:128]))
                        M.set(f"ogtev_{h}_{g}_{qtl}", q.n)
            M.set("sc_end", q.n)

        @block.vector
        def _(vector):
            q = Q(vector, q_ve)
            q.wait(q_sy, M["wload"])
            q.op(vector.tensor_scalar(w_q, w_q, ATT_SCALE, None, ALU.mult))
            M.set("wq_scaled", q.n)
            for n in range(NTC):
                for u in range(8):
                    q.wait(q_sc, M[f"pev_{n}_{u}"])
                    srcb = qraw[u % 2]
                    dst = qout[u % 2]
                    if n * 8 + u >= 2:
                        pz = n * 8 + u - 2
                        q.wait(q_te, M[f"ptT_{pz // 8}_{pz % 8}_1"])
                    q.wait(q_sy, M[f"csld_{n}"])
                    tt = u % 4
                    cs = cosb[n % 2][:, tt * 64:(tt + 1) * 64]
                    sn = sinb[n % 2][:, tt * 64:(tt + 1) * 64]
                    for h in range(2):
                        x1 = srcb[:, h * 128: h * 128 + 64]
                        x2 = srcb[:, h * 128 + 64: h * 128 + 128]
                        o1 = dst[:, h * 128: h * 128 + 64]
                        o2 = dst[:, h * 128 + 64: h * 128 + 128]
                        t1 = rsc[:, 0:64]
                        q.op(vector.tensor_tensor(o1, x1, cs, ALU.mult))
                        q.op(vector.tensor_tensor(t1, x2, sn, ALU.mult))
                        q.op(vector.tensor_sub(o1, o1, t1))
                        q.op(vector.tensor_tensor(o2, x2, cs, ALU.mult))
                        q.op(vector.tensor_tensor(t1, x1, sn, ALU.mult))
                        q.op(vector.tensor_add(o2, o2, t1))
                    M.set(f"rope_{n}_{u}", q.n)
            for n in range(NTC):
                q.wait(q_te, M[f"s2mm_{n}"])
                q.op(vector.tensor_scalar(rsc[:], pr_s2, bs2_sb[:], None, ALU.add))
                q.op(vector.tensor_scalar(rsc[:], rsc[:], -CLAMP, CLAMP,
                                          ALU.max, ALU.min))
                M.set(f"esv_{n}", q.n)
            q.wait(q_sc, M[f"esx_{NTC - 1}"])
            q.wait(q_te, M[f"Smm_{NTC - 1}"])
            q.op(vector.memset(ones_t, 1.0))
            q.op(vector.tensor_tensor_scan(cs_sb[:], ones_t, esT_sb[:], 0.0,
                                           ALU.mult, ALU.add))
            M.set("scan_done", q.n)
            q.op(vector.reciprocal(enzT_sb[:], cs_sb[:]))
            M.set("enz_done", q.n)
            for h in range(2):
                for g in range(NQG):
                    jmax = g * 8 + 8
                    M.set(f"Cgrp_free_ve_{h}_{g}", q.n)
                    for qtl in range(8):
                        q.wait(q_te, M[f"okmm_{h}_{g}_{jmax - 1}"])
                        q.wait(q_sy, M[f"Cg_{h}_{g}"])
                        q.op(vector.tensor_tensor(okf,
                                                  pr_ok[:, ts(qtl, 64)],
                                                  enzg[:, ts(qtl, 64)], ALU.mult))
                        q.op(vector.reduce_max(red[:, 0:1], okf, axis=AX.X,
                                               negate=True))
                        M.set(f"okv_{h}_{g}_{qtl}", q.n)
                        q.wait(q_sc, M[f"exp_{h}_{g}_{qtl}"])
                        q.op(vector.reduce_sum(red[:, 1:2], pexp, axis=AX.X))
                        q.op(vector.reciprocal(red[:, 2:3], red[:, 1:2]))
                        q.op(vector.tensor_scalar(pexp, pexp, red[:, 2:3],
                                                  None, ALU.mult))
                        q.op(vector.tensor_tensor(pexp, pexp,
                                                  enzg[:, ts(qtl, 64)], ALU.mult))
                        M.set(f"qve_{h}_{g}_{qtl}", q.n)
                    for qtl in range(8):
                        if qtl == 0:
                            q.wait(q_te, M[f"omm_{h}_{g}_{jmax - 1}"])
                        q.op(vector.tensor_copy(ocp[:, ts(qtl, 128)],
                                                pr_o[:, ts(qtl, 128)]))
                        q.op(vector.tensor_tensor(sqv, ocp[:, ts(qtl, 128)],
                                                  ocp[:, ts(qtl, 128)], ALU.mult))
                        q.op(vector.reduce_sum(red[:, 4 + qtl:5 + qtl], sqv,
                                               axis=AX.X))
                        q.op(vector.tensor_scalar(red[:, 4 + qtl:5 + qtl],
                                                  red[:, 4 + qtl:5 + qtl],
                                                  1.0 / 128.0, EPS, ALU.mult,
                                                  ALU.add))
                        M.set(f"msv_{h}_{g}_{qtl}", q.n)
                    for qtl in range(8):
                        q.wait(q_sc, M[f"rstd_{h}_{g}_{qtl}"])
                        if qtl > 0:
                            q.wait(q_te, M[f"ogT_{h}_{g}_{qtl - 1}"])
                        q.op(vector.tensor_scalar(ocp[:, ts(qtl, 128)],
                                                  ocp[:, ts(qtl, 128)],
                                                  red[:, 20 + qtl:21 + qtl],
                                                  None, ALU.mult))
                        q.op(vector.tensor_tensor(ocp[:, ts(qtl, 128)],
                                                  ocp[:, ts(qtl, 128)], gnwB[:],
                                                  ALU.mult))
                        q.wait(q_sc, M[f"sil_{h}_{g}_{qtl}"])
                        q.op(vector.tensor_tensor(ogv, ocp[:, ts(qtl, 128)],
                                                  silg, ALU.mult))
                        M.set(f"ogv_{h}_{g}_{qtl}", q.n)
                        M.set(f"outev_prevA_{h}_{g}_{qtl}", q.n)
                        q.wait(q_te, M[f"outmmA_{h}_{g}_{qtl}"])
                        q.wait(q_gp, M[f"outst_prev_{h}_{g}_{qtl}"])
                        q.op(vector.tensor_copy(outp_sb[:, 0:1024], pr_out[:]))
                        M.set(f"outevA_{h}_{g}_{qtl}", q.n)
                        q.wait(q_te, M[f"outmmB_{h}_{g}_{qtl}"])
                        q.op(vector.tensor_copy(outp_sb[:, 1024:2048], pr_out[:]))
                        M.set(f"outev_{h}_{g}_{qtl}", q.n)
            for i in range(NTT):
                q.wait(q_sy, M[f"Dld_{i}"])
                if i > 0:
                    q.wait(q_gp, M[f"xst_{i - 1}"])
                q.op(vector.tensor_add(xr, xr, rr))
                M.set(f"resadd_{i}", q.n)
            M.set("ve_end", q.n)

        @block.gpsimd
        def _(gpsimd):
            q = Q(gpsimd, q_gp)
            qc = Q(gpsimd, q_cc)
            qs = Q(gpsimd, q_gq)
            q.dma(ag_in.ap(), xnT[:, :])
            gpsimd.wait_ge(q_gp, q.n)
            qc.op(gpsimd.collective_compute(
                "AllGather", ALU.bypass, replica_groups=[list(range(NCORES))],
                ins=[ag_in.ap().opt()], outs=[ag_out.ap().opt()]))
            M.set("ag_done", qc.n)
            for n in range(NTC):
                for u in range(8):
                    for h in range(2):
                        q.wait(q_sc, M[f"ptev_{n}_{u}_{h}"])
                        dst = d_qT if u < 4 else d_kT
                        tt = u % 4
                        gt = n * 512 + tt * 128
                        q.dma(dst.ap()[h * 128:(h + 1) * 128, gt:gt + 128],
                              tr_ev[(n * 16 + u * 2 + h) % 4][:])
                        M.set(f"qkst_{n}_{u}_{h}", q.n)
                for u in range(8):
                    q.wait(q_sc, M[f"vgev_{n}_{u}"])
                    dst = d_v if u < 4 else d_g
                    tt = u % 4
                    q.dma(dst.ap()[n * 512 + tt * 128: n * 512 + (tt + 1) * 128,
                                   0:256], vg_ev[u % 2][:])
                    M.set(f"vgst_{n}_{u}", q.n)
            q.wait(q_sc, M[f"esx_{NTC - 1}"])
            q.dma(d_esT.ap()[:, :], esT_sb[:])
            M.set("esT_stored", q.n)
            for z, dst in enumerate([d_es, d_enz]):
                for jt in range(NJT):
                    q.wait(q_sc, M[f"trev_{z}_{jt}"])
                    q.dma(dst.ap()[ts(jt, 128), :], tr_ev[jt % 2][:])
                    M.set(f"trst_{z}_{jt}", q.n)
            for h in range(2):
                for g in range(NQG):
                    jmax = g * 8 + 8
                    qts = qts_of(g)
                    for j in qts:
                        qtl = j - g * 8
                        q.wait(q_sc, M[f"atev_{h}_{g}_{j}"])
                        qs.op(gpsimd.affine_select(
                            atg[:, qtl * 128:(qtl + 1) * 128],
                            atg[:, qtl * 128:(qtl + 1) * 128],
                            pattern=[[1, 128]], compare_op=ALU.is_ge,
                            fill=0.0, base=0, channel_multiplier=-1))
                        M.set(f"atsel_{h}_{g}_{j}", qs.n)
                    for j in qts:
                        qtl = j - g * 8
                        q.wait(q_sc, M[f"w2ev_{h}_{g}_{j}"])
                        qs.op(gpsimd.affine_select(
                            atg[:, qtl * 128:(qtl + 1) * 128],
                            atg[:, qtl * 128:(qtl + 1) * 128],
                            pattern=[[1, 128]], compare_op=ALU.is_ge,
                            fill=0.0, base=0, channel_multiplier=-1))
                        M.set(f"w2sel_{h}_{g}_{j}", qs.n)
                    for qtl in range(8):
                        M.set(f"outst_prev_{h}_{g}_{qtl}", q.n)
                        q.wait(q_ve, M[f"outev_{h}_{g}_{qtl}"])
                        qt = g * 8 + qtl
                        if h == 0:
                            q.dma(d_outp.ap()[ts(qt, 128), :], outp_sb)
                        else:
                            q.dma(d_outp.ap()[ts(qt, 128), :], outp_sb,
                                  accum_op=ALU.add)
                        M.set(f"outst_{h}_{g}_{qtl}", q.n)
            gpsimd.wait_ge(q_gp, q.n)
            qc.op(gpsimd.collective_compute(
                "ReduceScatter", ALU.add, replica_groups=[list(range(NCORES))],
                ins=[d_outp.ap().opt()], outs=[d_rs.ap().opt()]))
            M.set("rs_done", qc.n)
            for i in range(NTT):
                q.wait(q_ve, M[f"resadd_{i}"])
                q.dma(x_new[ts(i, 128), :], xr)
                M.set(f"xst_{i}", q.n)
            gpsimd.wait_ge(q_gp, q.n)
            M.set("gp_end", q.n)

    return st


def h_cols(u):
    tt = u % 4
    return slice(0, 256)


def build_l0attn(st: Stage):
    return twopass(emit_l0attn, st)


def _rope_tables():
    pos = np.arange(T, dtype=np.float64)
    inv_freq = 1.0 / (10000.0 ** (np.arange(0, 128, 2, dtype=np.float64) / 128.0))
    ang = np.outer(pos, inv_freq)
    return (np.ascontiguousarray(np.cos(ang).astype(np.float32)),
            np.ascontiguousarray(np.sin(ang).astype(np.float32)))


def run_l0attn(x_slices, xnT_slices, Wq, Wk, Wv, Wg, Wo, Ws1, Ws2, bs2, gnw):
    st = Stage("l0attn")
    build_l0attn(st)
    cosT, sinT = _rope_tables()
    ws1T = np.ascontiguousarray(Ws1.T)
    in_maps = []
    for c in range(NCORES):
        r = slice(2 * c * 128, (2 * c + 2) * 128)
        rm = slice(2 * c * 64, (2 * c + 2) * 64)
        in_maps.append({
            "x": np.ascontiguousarray(x_slices[c]),
            "xnT": np.ascontiguousarray(xnT_slices[c]),
            "WqT": np.ascontiguousarray(Wq[r, :].T),
            "WkT": np.ascontiguousarray(Wk[r, :].T),
            "WvT": np.ascontiguousarray(Wv[r, :].T),
            "WgT": np.ascontiguousarray(Wg[r, :].T),
            "Ws1T": ws1T,
            "Ws2T": np.ascontiguousarray(Ws2[rm, :].T),
            "bs2": np.ascontiguousarray(bs2[2 * c * 64:(2 * c + 2) * 64]
                                        .reshape(128, 1)),
            "gnw": np.ascontiguousarray(gnw.reshape(1, 128)),
            "WoT": np.ascontiguousarray(Wo[:, r].T),
            "cosTM": cosT, "sinTM": sinT,
            "ident": _IDENT, "ones_col": _ONES_COL,
        })
    res = st.run(in_maps)
    return [res[i]["x_new"] for i in range(NCORES)]



INTER = 8192
NMT = INTER // 128


def emit_l1ffn(st: Stage, M: Milestones):
    nc = st.nc
    x = st.inp("x", [TL, D])
    xnT = st.inp("xnT", [D, TL])
    wupT = st.inp("WupT", [D, INTER])
    wgtT = st.inp("WgtT", [D, INTER])
    wdnT = st.inp("WdnT", [INTER, D])
    su_ug = st.inp("su_ug", [128, NDC])
    ws_up = st.inp("ws_up", [128, NMT])
    ws_gate = st.inp("ws_gate", [128, NMT])
    su_dp = st.inp("su_dp", [128, NMT])
    ws_dp = st.inp("ws_dp", [1, D])
    h1c = st.inp("H128n1", [128, 128])
    h2c = st.inp("H128n2", [128, 128])
    ones_col = st.inp("ones_col", [1, 128])
    x_new = st.out("x_new", [TL, D])

    NTCH = TL // 512

    with ExitStack() as ctx:
        sb = lambda nm, s, d=F32: ctx.enter_context(nc.sbuf_tensor(nm, s, d))
        sem = lambda nm: ctx.enter_context(nc.semaphore(nm))
        mt = sb("mt", [128, NMT * 512])
        xh = sb("xh", [128, NDC * 512])
        xr4 = xh
        wst = [sb(f"wst{k}", [128, NDC * 128]) for k in range(2)]
        wdn = [sb(f"wdn{k}", [128, 512]) for k in range(2)]
        su_ug_sb = sb("su_ug_sb", [128, NDC])
        ws_up_sb = sb("ws_up_sb", [128, NMT])
        ws_gate_sb = sb("ws_gate_sb", [128, NMT])
        su_dp_sb = sb("su_dp_sb", [128, NMT])
        wsdp_row = sb("wsdp_row", [1, D])
        wsdpB = sb("wsdpB", [128, D])
        h1_sb = sb("h1_sb", [128, 128])
        h2_sb = sb("h2_sb", [128, 128])
        onc = sb("onc", [1, 128])
        upev = sb("upev", [128, 512])
        gtev = sb("gtev", [128, 512])
        silv = sb("silv", [128, 512])
        yv = sb("yv", [128, 512])
        tmp = yv
        ps_all = ctx.enter_context(nc.psum_tensor("ps_all", [128, 4096], F32))
        q_sy = sem("q_sy"); q_te = sem("q_te"); q_sc = sem("q_sc")
        q_ve = sem("q_ve"); q_gp = sem("q_gp")
        block = ctx.enter_context(nc.Block())

        pr_h = [ps_all[:, k * 512:(k + 1) * 512] for k in range(2)]
        pr_up = ps_all[:, 1024:1536]
        pr_gt = ps_all[:, 1536:2048]
        pr_dn = [ps_all[:, 2048 + k * 512: 2048 + (k + 1) * 512] for k in range(4)]
        pr_wb = ps_all[:, 1024:1536]

        def mtile(mi, tc_len=512):
            return mt[:, mi * 512: mi * 512 + tc_len]

        @block.sync
        def _(sync):
            q = Q(sync, q_sy)
            q.dma(su_ug_sb[:], su_ug[:, :])
            q.dma(ws_up_sb[:], ws_up[:, :])
            q.dma(ws_gate_sb[:], ws_gate[:, :])
            q.dma(su_dp_sb[:], su_dp[:, :])
            q.dma(wsdp_row[:], ws_dp[:, :])
            q.dma(h1_sb[:], h1c[:, :])
            q.dma(h2_sb[:], h2c[:, :])
            q.dma(onc[:], ones_col[:, :])
            M.set("wload", q.n)
            for tc in range(NTCH):
                if tc > 0:
                    q.wait(q_te, M[f"upgate_mm_done_{tc - 1}"])
                    q.wait(q_gp, M[f"xst_{tc - 1}_3"])
                q.dma(xh[:].rearrange("p (c t) -> p c t", c=NDC),
                      xnT[:, tc * 512:(tc + 1) * 512]
                      .rearrange("(c p) t -> p c t", p=128))
                M.set(f"xld_{tc}", q.n)
                for m in range(NMT):
                    for wi, wt in enumerate((wupT, wgtT)):
                        gi = (tc * NMT + m) * 2 + wi
                        if gi >= 2:
                            q.wait(q_te, M[f"ug_mm_{gi - 2}"])
                        q.dma(wst[wi][:].rearrange("p (c n) -> p c n", c=NDC),
                              wt[:, m * 128:(m + 1) * 128]
                              .rearrange("(c p) n -> p c n", p=128))
                        M.set(f"wld_{tc}_{m}_{wi}", q.n)
                for v in range(4):
                    q.wait(q_te, M[f"upgate_mm_done_{tc}"])
                    q.dma(xr4[:, v * D:(v + 1) * D],
                          x[tc * 512 + v * 128: tc * 512 + (v + 1) * 128, :])
                    M.set(f"xrld_{tc}_{v}", q.n)
                for o in range(4):
                    for k in range(NMT):
                        gi = o * NMT + k
                        if gi >= 2:
                            q.wait(q_te, M[f"dn_mm_{tc}_{gi - 2}"])
                        q.dma(wdn[gi % 2][:],
                              wdnT[k * 128:(k + 1) * 128,
                                   o * 512:(o + 1) * 512])
                        M.set(f"dld_{tc}_{o}_{k}", q.n)

        @block.tensor
        def _(tensor):
            q = Q(tensor, q_te)
            q.wait(q_sy, M["wload"])
            q.wait(q_ve, M["wsdp_scaled"])
            for cc in range(4):
                if cc > 0:
                    q.wait(q_sc, M[f"wbev_{cc - 1}"])
                q.op(tensor.matmul(pr_wb, onc[:], wsdp_row[:, ts(cc, 512)],
                                   start=True, stop=True), serial=False)
                M.set(f"wbmm_{cc}", q.n)
            for tc in range(NTCH):
                for c in range(NDC):
                    q.wait(q_ve, M[f"x1_{tc}"])
                    if c >= 2:
                        q.wait(q_sc, M[f"h1ev_{tc}_{c - 2}"])
                    elif tc > 0:
                        q.wait(q_sc, M[f"h1ev_{tc - 1}_{NDC - 2 + c}"])
                    q.op(tensor.matmul(pr_h[c % 2], h1_sb[:],
                                       xh[:, ts(c, 512)], start=True,
                                       stop=True), serial=False)
                    M.set(f"h1mm_{tc}_{c}", q.n)
                for m in range(NMT):
                    for wi in range(2):
                        gi = (tc * NMT + m) * 2 + wi
                        q.wait(q_sy, M[f"wld_{tc}_{m}_{wi}"])
                        if wi == 0:
                            q.wait(q_sc, M[f"h1ev_{tc}_{NDC - 1}"])
                        if gi >= 2:
                            q.wait(q_sc, M[f"ugev_{gi - 2}"])
                        else:
                            q.wait(q_sc, M["wbev_3"])
                        prd = pr_up if wi == 0 else pr_gt
                        for c in range(NDC):
                            q.op(tensor.matmul(prd, wst[wi][:, ts(c, 128)],
                                               xh[:, ts(c, 512)],
                                               start=(c == 0),
                                               stop=(c == NDC - 1)), serial=False)
                        M.set(f"ug_mm_{gi}", q.n)
                    M.set(f"ugmm_{tc}_{m}", q.n)
                M.set(f"upgate_mm_done_{tc}", q.n)
                for c in range(NMT):
                    q.wait(q_ve, M[f"x2_{tc}"])
                    if c >= 2:
                        q.wait(q_sc, M[f"h2ev_{tc}_{c - 2}"])
                    else:
                        q.wait(q_sc, M[f"h1ev_{tc}_{NDC - 2 + c}"])
                    q.op(tensor.matmul(pr_h[c % 2], h2_sb[:],
                                       mtile(c), start=True, stop=True),
                         serial=False)
                    M.set(f"h2mm_{tc}_{c}", q.n)
                for o in range(4):
                    for k in range(NMT):
                        gi = o * NMT + k
                        q.wait(q_sy, M[f"dld_{tc}_{o}_{k}"])
                        if k == 0:
                            q.wait(q_sc, M[f"h2ev_{tc}_{NMT - 1}"])
                            if o > 0:
                                q.wait(q_ve, M[f"ydn_{tc}_{o - 1}"])
                            elif tc > 0:
                                q.wait(q_ve, M[f"ydn_{tc - 1}_3"])
                        for v in range(4):
                            q.op(tensor.matmul(
                                pr_dn[v], mtile(k)[:, v * 128:(v + 1) * 128],
                                wdn[gi % 2][:], start=(k == 0),
                                stop=(k == NMT - 1)), serial=False)
                        M.set(f"dn_mm_{tc}_{gi}", q.n)
                    M.set(f"dnmm_{tc}_{o}", q.n)

        @block.scalar
        def _(scalar):
            q = Q(scalar, q_sc)
            for cc in range(4):
                q.wait(q_te, M[f"wbmm_{cc}"])
                q.op(scalar.copy(wsdpB[:, ts(cc, 512)], pr_wb))
                M.set(f"wbev_{cc}", q.n)
            for tc in range(NTCH):
                for c in range(NDC):
                    q.wait(q_te, M[f"h1mm_{tc}_{c}"])
                    q.op(scalar.copy(xh[:, ts(c, 512)], pr_h[c % 2]))
                    M.set(f"h1ev_{tc}_{c}", q.n)
                for m in range(NMT):
                    gi0 = (tc * NMT + m) * 2
                    q.wait(q_te, M[f"ug_mm_{gi0}"])
                    q.op(scalar.activation(upev[:], pr_up, AF.Copy,
                                           scale=ws_up_sb[:, m:m + 1]))
                    M.set(f"ugev_{gi0}", q.n)
                    q.wait(q_te, M[f"ug_mm_{gi0 + 1}"])
                    q.op(scalar.activation(gtev[:], pr_gt, AF.Copy,
                                           scale=ws_gate_sb[:, m:m + 1]))
                    M.set(f"ugev_{gi0 + 1}", q.n)
                    q.op(scalar.activation(silv[:], gtev[:], AF.Silu))
                    M.set(f"sil_{tc}_{m}", q.n)
                    q.wait(q_ve, M[f"mt_{tc}_{m}"])
                for c in range(NMT):
                    q.wait(q_te, M[f"h2mm_{tc}_{c}"])
                    q.op(scalar.copy(mtile(c), pr_h[c % 2]))
                    M.set(f"h2ev_{tc}_{c}", q.n)

        @block.vector
        def _(vector):
            q = Q(vector, q_ve)
            q.wait(q_sy, M["wload"])
            q.op(vector.tensor_scalar(wsdp_row[:], wsdp_row[:], 64.0, None,
                                      ALU.mult))
            q.op(vector.tensor_scalar(ws_up_sb[:], ws_up_sb[:], 64.0, None,
                                      ALU.mult))
            q.op(vector.tensor_scalar(ws_gate_sb[:], ws_gate_sb[:], 64.0, None,
                                      ALU.mult))
            M.set("wsdp_scaled", q.n)
            for tc in range(NTCH):
                q.wait(q_sy, M[f"xld_{tc}"])
                for c in range(NDC):
                    q.op(vector.tensor_scalar(xh[:, ts(c, 512)],
                                              xh[:, ts(c, 512)],
                                              su_ug_sb[:, c:c + 1], None,
                                              ALU.mult))
                for s in range(4):
                    step = 1 << s
                    for i in range(NDC):
                        if i & step:
                            continue
                        j = i + step
                        a = xh[:, ts(i, 512)]
                        b = xh[:, ts(j, 512)]
                        q.op(vector.tensor_sub(tmp[:], a, b))
                        q.op(vector.tensor_add(a, a, b))
                        q.op(vector.tensor_copy(b, tmp[:]))
                M.set(f"x1_{tc}", q.n)
                for m in range(NMT):
                    q.wait(q_sc, M[f"sil_{tc}_{m}"])
                    q.op(vector.tensor_tensor(yv[:], upev[:], silv[:], ALU.mult))
                    q.op(vector.tensor_scalar(mtile(m), yv[:],
                                              su_dp_sb[:, m:m + 1], None,
                                              ALU.mult))
                    M.set(f"mt_{tc}_{m}", q.n)
                for s in range(6):
                    step = 1 << s
                    for i in range(NMT):
                        if i & step:
                            continue
                        j = i + step
                        a = mtile(i)
                        b = mtile(j)
                        q.op(vector.tensor_sub(tmp[:], a, b))
                        q.op(vector.tensor_add(a, a, b))
                        q.op(vector.tensor_copy(b, tmp[:]))
                M.set(f"x2_{tc}", q.n)
                for o in range(4):
                    q.wait(q_te, M[f"dnmm_{tc}_{o}"])
                    for v in range(4):
                        if o == 0:
                            q.wait(q_sy, M[f"xrld_{tc}_{v}"])
                        xrv = xr4[:, v * D + o * 512: v * D + (o + 1) * 512]
                        q.op(vector.tensor_tensor(yv[:], pr_dn[v],
                                                  wsdpB[:, ts(o, 512)],
                                                  ALU.mult))
                        q.op(vector.scalar_tensor_tensor(
                            xrv, yv[:], 0.5, xrv, ALU.mult, ALU.add))
                    M.set(f"ydn_{tc}_{o}", q.n)
                for v in range(4):
                    M.set(f"res_{tc}_{v}", q.n)

        @block.gpsimd
        def _(gpsimd):
            q = Q(gpsimd, q_gp)
            for tc in range(NTCH):
                for v in range(4):
                    q.wait(q_ve, M[f"res_{tc}_{v}"])
                    q.dma(x_new[tc * 512 + v * 128: tc * 512 + (v + 1) * 128, :],
                          xr4[:, v * D:(v + 1) * D])
                    M.set(f"xst_{tc}_{v}", q.n)
            gpsimd.wait_ge(q_gp, q.n)
    return st


def build_l1ffn(st: Stage):
    return twopass(emit_l1ffn, st)


def _hadamard(n):
    h = np.array([[1.0]], dtype=np.float64)
    while h.shape[0] < n:
        h = np.block([[h, h], [h, -h]])
    return h


def run_l1ffn(x_slices, xnT_slices, SU_ug, SU_dp, Wscale_ug, Wscale_dp,
              W_up, W_gate, W_down):
    st = Stage("l1ffn")
    build_l1ffn(st)
    h128 = _hadamard(128)
    h1 = np.ascontiguousarray((h128 / (math.sqrt(2048.0) * 64.0))
                              .astype(np.float32))
    h2 = np.ascontiguousarray((h128 / (math.sqrt(8192.0) * 64.0))
                              .astype(np.float32))
    base = {
        "WupT": np.ascontiguousarray(W_up.T),
        "WgtT": np.ascontiguousarray(W_gate.T),
        "WdnT": np.ascontiguousarray(W_down.T),
        "su_ug": np.ascontiguousarray(SU_ug.reshape(NDC, 128).T),
        "ws_up": np.ascontiguousarray(Wscale_ug[:INTER].reshape(NMT, 128).T),
        "ws_gate": np.ascontiguousarray(Wscale_ug[INTER:].reshape(NMT, 128).T),
        "su_dp": np.ascontiguousarray(SU_dp.reshape(NMT, 128).T),
        "ws_dp": np.ascontiguousarray(Wscale_dp.reshape(1, D)),
        "H128n1": h1, "H128n2": h2, "ones_col": _ONES_COL,
    }
    in_maps = []
    for i in range(NCORES):
        m = dict(base)
        m["x"] = np.ascontiguousarray(x_slices[i])
        m["xnT"] = np.ascontiguousarray(xnT_slices[i])
        in_maps.append(m)
    res = st.run(in_maps)
    return [res[i]["x_new"] for i in range(NCORES)]



D_INNER = 4096
D_STATE = 16
CHL = D_INNER // NCORES
NCHT = CHL // 128
NTQ = 4
TQ = T // NTQ


def emit_mamba(st: Stage, M: Milestones):
    nc = st.nc
    x = st.inp("x", [TL, D])
    xnT = st.inp("xnT", [D, TL])
    winT = st.inp("WinT", [D, 2 * CHL])
    convw = st.inp("convw", [128, NCHT * 4])
    convb = st.inp("convb", [128, NCHT])
    xpwT = st.inp("xpwT", [CHL, 160])
    dtwT = st.inp("dtwT", [128, CHL])
    dtb = st.inp("dtb", [128, NCHT])
    alog = st.inp("Alog", [128, NCHT * D_STATE])
    dpar = st.inp("Dpar", [128, NCHT])
    moT = st.inp("moT", [CHL, D]).tensor
    ones_col = st.inp("ones_col", [1, 128])
    h_new = st.out("h_new", [TL, D])

    ag_in = st.scratch("ag_in", [D, TL])
    ag_out = st.scratch("ag_out", [NCORES * D, TL])
    d_xz = st.scratch("d_xz", [2 * CHL, T])
    d_xc = st.scratch("d_xc", [CHL, T])
    d_dt = st.scratch("d_dt", [CHL, T])
    d_dbp = st.scratch("d_dbp", [160, T])
    d_dbc = st.scratch("d_dbc", [160, T])
    d_outp = st.scratch("d_outp", [T, D])
    d_rs = st.scratch("d_rs", [TL, D])

    cpr = max(1, TL // 512)
    def hs_chunk(c, n):
        r, tl = n // cpr, (n % cpr) * 512
        return ag_out.ap()[r * D + c * 128: r * D + (c + 1) * 128, tl: tl + 512]

    NTC_ = T // 512
    NTSUB = TQ // 128
    NDT = NTC_ * 4

    with ExitStack() as ctx:
        sb = lambda nm, s, d=F32: ctx.enter_context(nc.sbuf_tensor(nm, s, d))
        sem = lambda nm: ctx.enter_context(nc.semaphore(nm))
        arena = sb("arena", [128, 48128])
        xc4 = arena[:, 0:8192]
        dt4 = arena[:, 8192:16384]
        xt4 = arena[:, 16384:24576]
        ya4 = arena[:, 24576:32768]
        Bb = arena[:, 32768:34816]
        Cb = arena[:, 34816:36864]
        dAv = arena[:, 36864:38912]
        uv = arena[:, 38912:40960]
        ssv = arena[:, 40960:43008]
        silz = arena[:, 43008:45056]
        mo_sb = [arena[:, 45056 + k * 512: 45056 + (k + 1) * 512] for k in range(4)]
        opev = [arena[:, 47104 + k * 512: 47104 + (k + 1) * 512] for k in range(2)]
        winT_sb = arena[:, 0:16384]
        hsb = [arena[:, 16384 + k * 512: 16384 + (k + 1) * 512] for k in range(2)]
        ipev = [arena[:, 17408 + k * 512: 17408 + (k + 1) * 512] for k in range(2)]
        xpad = arena[:, 0:8256]
        xcv = arena[:, 8256:16448]
        xpl = [arena[:, 18432 + k * 512: 18432 + (k + 1) * 512] for k in range(2)]
        brow = arena[0:1, 0:2 * TQ]
        dbev = arena[:, 19456:20480]
        dtev = [arena[:, 20480 + k * 512: 20480 + (k + 1) * 512] for k in range(2)]
        xr = arena[:, 0:2048]
        rr = arena[:, 2048:4096]
        xpw_sb = sb("xpw_sb", [128, 4 * 160])
        dtw_sb = sb("dtw_sb", [128, CHL])
        dtb_sb = sb("dtb_sb", [128, NCHT])
        convw_sb = sb("convw_sb", [128, NCHT * 4])
        convb_sb = sb("convb_sb", [128, NCHT])
        alog_sb = sb("alog_sb", [128, NCHT * D_STATE])
        an_sb = sb("an_sb", [128, NCHT * D_STATE])
        dpar_sb = sb("dpar_sb", [128, NCHT])
        onc = sb("onc", [1, 128])
        carry = sb("carry", [128, NCHT * D_STATE])
        t1 = sb("t1", [128, 512])
        t2 = sb("t2", [128, 512])
        ps_all = ctx.enter_context(nc.psum_tensor("ps_all", [128, 4096], F32))
        q_sy = sem("q_sy"); q_te = sem("q_te"); q_sc = sem("q_sc")
        q_ve = sem("q_ve"); q_gp = sem("q_gp"); q_cc = sem("q_cc")
        block = ctx.enter_context(nc.Block())

        pr_ip = [ps_all[:, k * 512:(k + 1) * 512] for k in range(8)]
        pr_xp = ps_all[:, 0:512]
        pr_bc = ps_all[0:32, 512:1024]
        pr_dtm = [ps_all[:, (4 + m) * 512:(5 + m) * 512] for m in range(4)]
        pr_bcast = [ps_all[:, k * 512:(k + 1) * 512] for k in range(8)]
        pr_op = [ps_all[:, (4 + o) * 512:(5 + o) * 512] for o in range(4)]

        def acol(cht, n):
            return an_sb[:, cht * D_STATE + n: cht * D_STATE + n + 1]

        @block.sync
        def _(sync):
            q = Q(sync, q_sy)
            q.dma(xpw_sb[:].rearrange("p (c n) -> p c n", c=4),
                  xpwT.rearrange("(c p) n -> p c n", p=128))
            q.dma(dtw_sb[:], dtwT[:, :])
            q.dma(dtb_sb[:], dtb[:, :])
            q.dma(convw_sb[:], convw[:, :])
            q.dma(convb_sb[:], convb[:, :])
            q.dma(alog_sb[:], alog[:, :])
            q.dma(dpar_sb[:], dpar[:, :])
            q.dma(onc[:], ones_col[:, :])
            M.set("wload", q.n)
            q.wait(q_cc, M["ag_done"])
            q.dma(winT_sb.rearrange("p (c m) -> p c m", c=NDC),
                  winT.rearrange("(c p) m -> p c m", p=128))
            M.set("winld", q.n)
            for n in range(NTC_):
                for c in range(NDC):
                    gi = n * NDC + c
                    if gi >= 2:
                        q.wait(q_te, M[f"ipmm_{gi - 2}"])
                    q.dma(hsb[gi % 2], hs_chunk(c, n))
                    M.set(f"Bld_{n}_{c}", q.n)
            for cht in range(NCHT):
                if cht == 0:
                    q.wait(q_te, M["ip_done"])
                    q.wait(q_gp, M[f"ipst_{NTC_ - 1}_7"])
                else:
                    q.wait(q_gp, M[f"xcst_{cht - 1}"])
                q.dma(xpad[:, 3:3 + T], d_xz.ap()[cht * 128:(cht + 1) * 128, :])
                M.set(f"cvld_{cht}", q.n)
            for n in range(NTC_):
                for c in range(4):
                    gi = n * 4 + c
                    if gi >= 2:
                        q.wait(q_te, M[f"xpmm_{gi - 2}"])
                    else:
                        q.wait(q_gp, M[f"xcst_{NCHT - 1}"])
                    q.dma(xpl[gi % 2], d_xc.ap()[c * 128:(c + 1) * 128,
                                                 n * 512:(n + 1) * 512])
                    M.set(f"xpld_{n}_{c}", q.n)
            for n in range(NTC_):
                if n >= 2:
                    q.wait(q_te, M[f"dtmm_n_{n - 2}"])
                else:
                    q.wait(q_te, M["xp_done"])
                    q.wait(q_cc, M["ar_done"])
                q.dma(xpl[n % 2], d_dbc.ap()[0:128, n * 512:(n + 1) * 512])
                M.set(f"dtld_{n}", q.n)
            for tq in range(NTQ):
                for cht in range(NCHT):
                    if tq == 0 and cht == 0:
                        q.wait(q_gp, M["dt_all_stored"])
                    if tq > 0 and cht == 0:
                        q.wait(q_te, M[f"opmm_{tq - 1}_{NTSUB - 1}_3"])
                        q.wait(q_ve, M[f"ygate_{tq - 1}_{NCHT - 1}"])
                    q.dma(xc4[:, cht * TQ:(cht + 1) * TQ],
                          d_xc.ap()[cht * 128:(cht + 1) * 128,
                                    tq * TQ:(tq + 1) * TQ])
                    q.dma(dt4[:, cht * TQ:(cht + 1) * TQ],
                          d_dt.ap()[cht * 128:(cht + 1) * 128,
                                    tq * TQ:(tq + 1) * TQ])
                    M.set(f"scld_{tq}_{cht}", q.n)
                for n in range(D_STATE):
                    if n > 0:
                        q.wait(q_te, M[f"bcmm_{tq}_{n - 1}"])
                    else:
                        q.wait(q_ve, M[f"prep_{tq}_{NCHT - 1}"])
                        if tq > 0:
                            q.wait(q_te, M[f"bcmm_{tq - 1}_{D_STATE - 1}"])
                    q.dma(brow[:, 0:TQ], d_dbc.ap()[128 + n:129 + n,
                                                    tq * TQ:(tq + 1) * TQ])
                    q.dma(brow[:, TQ:2 * TQ], d_dbc.ap()[144 + n:145 + n,
                                                         tq * TQ:(tq + 1) * TQ])
                    M.set(f"bcld_{tq}_{n}", q.n)
                for cht in range(NCHT):
                    if tq + cht > 0:
                        pv = (tq, cht - 1) if cht > 0 else (tq - 1, NCHT - 1)
                        q.wait(q_ve, M[f"ygate_{pv[0]}_{pv[1]}"])
                    q.dma(silz[:, 0:TQ],
                          d_xz.ap()[CHL + cht * 128: CHL + (cht + 1) * 128,
                                    tq * TQ:(tq + 1) * TQ])
                    M.set(f"zld_{tq}_{cht}", q.n)
                for tsub in range(NTSUB):
                    for o in range(4):
                        for cht in range(NCHT):
                            gi = (tsub * 4 + o) * NCHT + cht
                            if tq * NTSUB * 16 + gi >= 4:
                                pz = tq * NTSUB * 16 + gi - 4
                                ptq, prem = pz // (NTSUB * 16), pz % (NTSUB * 16)
                                q.wait(q_te, M[f"opmm_{ptq}_{prem // 16}_"
                                               f"{(prem % 16) // 4}"])
                            q.dma(mo_sb[gi % 4],
                                  moT.ap()[cht * 128:(cht + 1) * 128,
                                           o * 512:(o + 1) * 512])
                            M.set(f"mold_{tq}_{tsub}_{o}_{cht}", q.n)
            for i in range(NTT):
                q.wait(q_cc, M["rs_done"])
                if i > 0:
                    q.wait(q_ve, M[f"resadd_{i - 1}"])
                    q.wait(q_gp, M[f"xst_{i - 1}"])
                q.dma(xr[:, 0:D], x[ts(i, 128), :])
                q.dma(rr[:, 0:D], d_rs.ap()[ts(i, 128), :])
                M.set(f"Dld_{i}", q.n)

        @block.tensor
        def _(tensor):
            q = Q(tensor, q_te)
            q.wait(q_sy, M["winld"])
            for n in range(NTC_):
                if n > 0:
                    q.wait(q_sc, M[f"ipev_{n - 1}_done"])
                for c in range(NDC):
                    gi = n * NDC + c
                    q.wait(q_sy, M[f"Bld_{n}_{c}"])
                    for m in range(8):
                        q.op(tensor.matmul(
                            pr_ip[m],
                            winT_sb[:, c * 1024 + m * 128:
                                    c * 1024 + (m + 1) * 128],
                            hsb[gi % 2], start=(c == 0), stop=(c == NDC - 1)),
                            serial=False)
                    M.set(f"ipmm_{gi}", q.n)
            M.set("ip_done", q.n)
            for n in range(NTC_):
                if n > 0:
                    q.wait(q_sc, M[f"dbev_{n - 1}"])
                else:
                    q.wait(q_sc, M[f"ipev_{NTC_ - 1}_done"])
                for c in range(4):
                    gi = n * 4 + c
                    q.wait(q_sy, M[f"xpld_{n}_{c}"])
                    q.op(tensor.matmul(pr_xp,
                                       xpw_sb[:, c * 160: c * 160 + 128],
                                       xpl[gi % 2], start=(c == 0),
                                       stop=(c == 3)), serial=False)
                    q.op(tensor.matmul(pr_bc,
                                       xpw_sb[:, c * 160 + 128:(c + 1) * 160],
                                       xpl[gi % 2], start=(c == 0),
                                       stop=(c == 3),
                                       skip_group_check=True), serial=False)
                    M.set(f"xpmm_{gi}", q.n)
                M.set(f"xpmm_n_{n}", q.n)
            M.set("xp_done", q.n)
            for n in range(NTC_):
                q.wait(q_sy, M[f"dtld_{n}"])
                for m in range(4):
                    gi = n * 4 + m
                    if gi >= 4:
                        q.wait(q_ve, M[f"dtb_{gi - 4}"])
                    q.op(tensor.matmul(pr_dtm[m],
                                       dtw_sb[:, m * 128:(m + 1) * 128],
                                       xpl[n % 2], start=True, stop=True),
                         serial=False)
                    M.set(f"dtmm_{gi}", q.n)
                M.set(f"dtmm_n_{n}", q.n)
            for tq in range(NTQ):
                for n in range(D_STATE):
                    q.wait(q_sy, M[f"bcld_{tq}_{n}"])
                    if tq == 0 and n == 0:
                        q.wait(q_ve, M[f"dtb_{NDT - 1}"])
                    if n > 0:
                        q.wait(q_sc, M[f"bcev_{tq}_{n - 1}"])
                    elif tq > 0:
                        q.wait(q_sc, M[f"bcev_{tq - 1}_{D_STATE - 1}"])
                        q.wait(q_ve, M[f"opev_{tq - 1}_{NTSUB - 1}_3"])
                    nbk = max(1, TQ // 512)
                    w_ = min(512, TQ)
                    for k in range(2 * nbk):
                        src_s = (brow[0:1, k * w_:(k + 1) * w_] if k < nbk else
                                 brow[0:1, TQ + (k - nbk) * w_:
                                      TQ + (k - nbk + 1) * w_])
                        q.op(tensor.matmul(pr_bcast[k][:, 0:w_], onc[:],
                                           src_s, start=True, stop=True),
                             serial=False)
                    M.set(f"bcmm_{tq}_{n}", q.n)
                for tsub in range(NTSUB):
                    for o in range(4):
                        if tsub == 0 and o == 0:
                            q.wait(q_sc, M[f"bcev_{tq}_{D_STATE - 1}"])
                            q.wait(q_ve, M[f"ygate_{tq}_{NCHT - 1}"])
                        if tsub > 0:
                            q.wait(q_ve, M[f"opev_{tq}_{tsub - 1}_{o}"])
                        elif tq > 0:
                            q.wait(q_ve, M[f"opev_{tq - 1}_{NTSUB - 1}_{o}"])
                        for cht in range(NCHT):
                            gi = (tsub * 4 + o) * NCHT + cht
                            q.wait(q_sy, M[f"mold_{tq}_{tsub}_{o}_{cht}"])
                            q.op(tensor.matmul(
                                pr_op[o],
                                ya4[:, cht * TQ + tsub * 128:
                                    cht * TQ + (tsub + 1) * 128],
                                mo_sb[gi % 4], start=(cht == 0),
                                stop=(cht == 3)), serial=False)
                        M.set(f"opmm_{tq}_{tsub}_{o}", q.n)
            M.set("te_end", q.n)

        @block.scalar
        def _(scalar):
            q = Q(scalar, q_sc)
            q.wait(q_sy, M["wload"])
            q.op(scalar.activation(an_sb[:], alog_sb[:], AF.Exp))
            q.op(scalar.mul(an_sb[:], an_sb[:], -1.0))
            M.set("an_done", q.n)
            for n in range(NTC_):
                q.wait(q_te, M[f"ipmm_{n * NDC + NDC - 1}"])
                for m in range(8):
                    gi = n * 8 + m
                    if gi >= 2:
                        q.wait(q_gp, M[f"ipst_{(gi - 2) // 8}_{(gi - 2) % 8}"])
                    q.op(scalar.copy(ipev[gi % 2], pr_ip[m]))
                    M.set(f"ipev_{n}_{m}", q.n)
                M.set(f"ipev_{n}_done", q.n)
            for cht in range(NCHT):
                q.wait(q_ve, M[f"convacc_{cht}"])
                q.op(scalar.activation(xpad[:, 0:T], xcv[:, 0:T], AF.Silu,
                                       bias=convb_sb[:, cht:cht + 1]))
                M.set(f"convsilu_{cht}", q.n)
            for n in range(NTC_):
                q.wait(q_te, M[f"xpmm_n_{n}"])
                if n > 0:
                    q.wait(q_gp, M[f"dbst_{n - 1}"])
                q.op(scalar.copy(dbev[:, 0:512], pr_xp))
                q.op(scalar.copy(dbev[0:32, 512:1024], pr_bc))
                M.set(f"dbev_{n}", q.n)
            for gi in range(NDT):
                q.wait(q_ve, M[f"dtb_{gi}"])
                q.op(scalar.activation(t2[:], t1[:], AF.Abs))
                q.op(scalar.activation(t2[:], t2[:], AF.Exp, scale=-1.0))
                q.op(scalar.activation(t2[:], t2[:], AF.Ln, bias=1.0))
                M.set(f"dtsp_{gi}", q.n)
                q.wait(q_ve, M[f"dtfin_{gi}"])
            for tq in range(NTQ):
                for n in range(D_STATE):
                    q.wait(q_te, M[f"bcmm_{tq}_{n}"])
                    if n > 0:
                        q.wait(q_ve, M[f"yacc_{tq}_{n - 1}_{NCHT - 1}"])
                    elif tq > 0:
                        q.wait(q_ve, M[f"yacc_{tq - 1}_{D_STATE - 1}_"
                                       f"{NCHT - 1}"])
                    nbk = max(1, TQ // 512)
                    w_ = min(512, TQ)
                    for k in range(nbk):
                        q.op(scalar.copy(Bb[:, k * w_:(k + 1) * w_],
                                         pr_bcast[k][:, 0:w_]))
                    for k in range(nbk):
                        q.op(scalar.copy(Cb[:, k * w_:(k + 1) * w_],
                                         pr_bcast[nbk + k][:, 0:w_]))
                    M.set(f"bcev_{tq}_{n}", q.n)
                    for cht in range(NCHT):
                        q.wait(q_ve, M[f"dAr_{tq}_{n}_{cht}"])
                        q.op(scalar.activation(dAv[:, 0:TQ], ssv[:, 0:TQ],
                                               AF.Exp))
                        M.set(f"dAx_{tq}_{n}_{cht}", q.n)
                        q.wait(q_ve, M[f"scan_{tq}_{n}_{cht}"])
                for cht in range(NCHT):
                    q.wait(q_sy, M[f"zld_{tq}_{cht}"])
                    q.op(scalar.activation(dAv[:, 0:TQ], silz[:, 0:TQ],
                                           AF.Silu))
                    M.set(f"zsil_{tq}_{cht}", q.n)
                    q.wait(q_ve, M[f"ygate_{tq}_{cht}"])
            M.set("sc_end", q.n)

        @block.vector
        def _(vector):
            q = Q(vector, q_ve)
            q.op(vector.memset(carry[:], 0.0))
            M.set("carry0", q.n)
            for cht in range(NCHT):
                q.wait(q_sy, M[f"cvld_{cht}"])
                q.op(vector.memset(xpad[:, 0:3], 0.0))
                q.op(vector.tensor_scalar(xcv[:, 0:T], xpad[:, 0:T],
                                          convw_sb[:, cht * 4:cht * 4 + 1],
                                          None, ALU.mult))
                for i in range(1, 4):
                    q.op(vector.scalar_tensor_tensor(
                        xcv[:, 0:T], xpad[:, i:i + T],
                        convw_sb[:, cht * 4 + i:cht * 4 + i + 1],
                        xcv[:, 0:T], ALU.mult, ALU.add))
                M.set(f"convacc_{cht}", q.n)
                q.wait(q_sc, M[f"convsilu_{cht}"])
            for gi in range(NDT):
                n, m = gi // 4, gi % 4
                q.wait(q_te, M[f"dtmm_{gi}"])
                if gi > 0:
                    q.wait(q_sc, M[f"dtsp_{gi - 1}"])
                q.op(vector.tensor_scalar(t1[:], pr_dtm[m], dtb_sb[:, m:m + 1],
                                          None, ALU.add))
                M.set(f"dtb_{gi}", q.n)
                q.wait(q_sc, M[f"dtsp_{gi}"])
                if gi >= 2:
                    q.wait(q_gp, M[f"dtst_{gi - 2}"])
                q.op(vector.tensor_scalar(dtev[gi % 2], t1[:], 0.0, None,
                                          ALU.max))
                q.op(vector.tensor_tensor(dtev[gi % 2], dtev[gi % 2], t2[:],
                                          ALU.add))
                M.set(f"dtfin_{gi}", q.n)
            for tq in range(NTQ):
                for cht in range(NCHT):
                    q.wait(q_sy, M[f"scld_{tq}_{cht}"])
                    if tq == 0 and cht == 0:
                        q.wait(q_sc, M["an_done"])
                    q.op(vector.tensor_tensor(xt4[:, cht * TQ:(cht + 1) * TQ],
                                              xc4[:, cht * TQ:(cht + 1) * TQ],
                                              dt4[:, cht * TQ:(cht + 1) * TQ],
                                              ALU.mult))
                    q.op(vector.tensor_scalar(ya4[:, cht * TQ:(cht + 1) * TQ],
                                              xc4[:, cht * TQ:(cht + 1) * TQ],
                                              dpar_sb[:, cht:cht + 1], None,
                                              ALU.mult))
                    M.set(f"prep_{tq}_{cht}", q.n)
                for n in range(D_STATE):
                    for cht in range(NCHT):
                        cs = slice(cht * TQ, (cht + 1) * TQ)
                        if n == 0:
                            q.wait(q_sc, M[f"bcev_{tq}_{n}"])
                        q.op(vector.tensor_scalar(ssv[:, 0:TQ], dt4[:, cs],
                                                  acol(cht, n), None, ALU.mult))
                        M.set(f"dAr_{tq}_{n}_{cht}", q.n)
                        q.wait(q_sc, M[f"bcev_{tq}_{n}"])
                        q.op(vector.tensor_tensor(uv[:, 0:TQ], xt4[:, cs],
                                                  Bb[:, 0:TQ], ALU.mult))
                        q.wait(q_sc, M[f"dAx_{tq}_{n}_{cht}"])
                        ccol = carry[:, cht * D_STATE + n:
                                     cht * D_STATE + n + 1]
                        q.op(vector.tensor_tensor_scan(ssv[:, 0:TQ],
                                                       dAv[:, 0:TQ],
                                                       uv[:, 0:TQ], ccol,
                                                       ALU.mult, ALU.add))
                        q.op(vector.tensor_copy(ccol, ssv[:, TQ - 1:TQ]))
                        M.set(f"scan_{tq}_{n}_{cht}", q.n)
                        q.op(vector.tensor_tensor(uv[:, 0:TQ], ssv[:, 0:TQ],
                                                  Cb[:, 0:TQ], ALU.mult))
                        q.op(vector.tensor_tensor(ya4[:, cs], ya4[:, cs],
                                                  uv[:, 0:TQ], ALU.add))
                        M.set(f"yacc_{tq}_{n}_{cht}", q.n)
                for cht in range(NCHT):
                    cs = slice(cht * TQ, (cht + 1) * TQ)
                    q.wait(q_sc, M[f"zsil_{tq}_{cht}"])
                    q.op(vector.tensor_tensor(ya4[:, cs], ya4[:, cs],
                                              dAv[:, 0:TQ], ALU.mult))
                    M.set(f"ygate_{tq}_{cht}", q.n)
                for tsub in range(NTSUB):
                    for o in range(4):
                        gi = tq * NTSUB * 4 + tsub * 4 + o
                        q.wait(q_te, M[f"opmm_{tq}_{tsub}_{o}"])
                        if gi >= 2:
                            pz = gi - 2
                            q.wait(q_gp, M[f"opst_{pz}"])
                        q.op(vector.tensor_copy(opev[gi % 2], pr_op[o]))
                        M.set(f"opev_{tq}_{tsub}_{o}", q.n)
            for i in range(NTT):
                q.wait(q_sy, M[f"Dld_{i}"])
                q.op(vector.tensor_add(xr[:, 0:D], xr[:, 0:D], rr[:, 0:D]))
                M.set(f"resadd_{i}", q.n)
            M.set("ve_end", q.n)

        @block.gpsimd
        def _(gpsimd):
            q = Q(gpsimd, q_gp)
            qc = Q(gpsimd, q_cc)
            q.dma(ag_in.ap(), xnT[:, :])
            gpsimd.wait_ge(q_gp, q.n)
            qc.op(gpsimd.collective_compute(
                "AllGather", ALU.bypass, replica_groups=[list(range(NCORES))],
                ins=[ag_in.ap().opt()], outs=[ag_out.ap().opt()]))
            M.set("ag_done", qc.n)
            for n in range(NTC_):
                for m in range(8):
                    gi = n * 8 + m
                    q.wait(q_sc, M[f"ipev_{n}_{m}"])
                    q.dma(d_xz.ap()[m * 128:(m + 1) * 128,
                                    n * 512:(n + 1) * 512], ipev[gi % 2])
                    M.set(f"ipst_{n}_{m}", q.n)
            for cht in range(NCHT):
                q.wait(q_sc, M[f"convsilu_{cht}"])
                q.dma(d_xc.ap()[cht * 128:(cht + 1) * 128, :], xpad[:, 0:T])
                M.set(f"xcst_{cht}", q.n)
            for n in range(NTC_):
                q.wait(q_sc, M[f"dbev_{n}"])
                q.dma(d_dbp.ap()[0:128, n * 512:(n + 1) * 512], dbev[:, 0:512])
                q.dma(d_dbp.ap()[128:160, n * 512:(n + 1) * 512],
                      dbev[0:32, 512:1024])
                M.set(f"dbst_{n}", q.n)
            gpsimd.wait_ge(q_gp, q.n)
            qc.op(gpsimd.collective_compute(
                "AllReduce", ALU.add, replica_groups=[list(range(NCORES))],
                ins=[d_dbp.ap().opt()], outs=[d_dbc.ap().opt()]))
            M.set("ar_done", qc.n)
            for gi in range(NDT):
                n, m = gi // 4, gi % 4
                q.wait(q_ve, M[f"dtfin_{gi}"])
                q.dma(d_dt.ap()[m * 128:(m + 1) * 128,
                                n * 512:(n + 1) * 512], dtev[gi % 2])
                M.set(f"dtst_{gi}", q.n)
            M.set("dt_all_stored", q.n)
            for tq in range(NTQ):
                for tsub in range(NTSUB):
                    for o in range(4):
                        gi = tq * NTSUB * 4 + tsub * 4 + o
                        q.wait(q_ve, M[f"opev_{tq}_{tsub}_{o}"])
                        q.dma(d_outp.ap()[tq * TQ + tsub * 128:
                                          tq * TQ + (tsub + 1) * 128,
                                          o * 512:(o + 1) * 512],
                              opev[gi % 2])
                        M.set(f"opst_{gi}", q.n)
            gpsimd.wait_ge(q_gp, q.n)
            qc.op(gpsimd.collective_compute(
                "ReduceScatter", ALU.add, replica_groups=[list(range(NCORES))],
                ins=[d_outp.ap().opt()], outs=[d_rs.ap().opt()]))
            M.set("rs_done", qc.n)
            for i in range(NTT):
                q.wait(q_ve, M[f"resadd_{i}"])
                q.dma(h_new[ts(i, 128), :], xr[:, 0:D])
                M.set(f"xst_{i}", q.n)
            gpsimd.wait_ge(q_gp, q.n)
    return st


def build_mamba(st: Stage):
    return twopass(emit_mamba, st)


def run_mamba(x_slices, xnT_slices, in_proj_w, conv_w, conv_b, x_proj_w,
              dt_proj_w, dt_proj_b, A_log, D_param, mamba_out_w):
    st = Stage("mamba")
    build_mamba(st)
    in_maps = []
    for c in range(NCORES):
        r = slice(c * CHL, (c + 1) * CHL)
        win = np.concatenate([in_proj_w[r], in_proj_w[D_INNER + c * CHL:
                                                      D_INNER + (c + 1) * CHL]], 0)
        in_maps.append({
            "x": np.ascontiguousarray(x_slices[c]),
            "xnT": np.ascontiguousarray(xnT_slices[c]),
            "WinT": np.ascontiguousarray(win.T),
            "convw": np.ascontiguousarray(
                conv_w[r].reshape(NCHT, 128, 4).transpose(1, 0, 2)
                .reshape(128, NCHT * 4)),
            "convb": np.ascontiguousarray(conv_b[r].reshape(NCHT, 128).T),
            "xpwT": np.ascontiguousarray(x_proj_w[:, r].T),
            "dtwT": np.ascontiguousarray(dt_proj_w[r].T),
            "dtb": np.ascontiguousarray(dt_proj_b[r].reshape(NCHT, 128).T),
            "Alog": np.ascontiguousarray(
                A_log[r].reshape(NCHT, 128, D_STATE).transpose(1, 0, 2)
                .reshape(128, NCHT * D_STATE)),
            "Dpar": np.ascontiguousarray(D_param[r].reshape(NCHT, 128).T),
            "moT": np.ascontiguousarray(mamba_out_w[:, r].T),
            "ones_col": _ONES_COL,
        })
    res = st.run(in_maps)
    return [res[i]["h_new"] for i in range(NCORES)]



WINDOW = 2048
LAMBDA_INIT = 0.8 - 0.6 * math.exp(-0.3)
NBAND = WINDOW // 128 + 1


def emit_diffattn(st: Stage, M: Milestones):
    nc = st.nc
    x = st.inp("x", [TL, D])
    xnT = st.inp("xnT", [D, TL])
    wqT = st.inp("WqT", [D, 256])
    wkT = st.inp("WkT", [D, 256])
    wvT = st.inp("WvT", [D, 256])
    bq = st.inp("bq", [128, 2])
    bk = st.inp("bk", [128, 2])
    bv = st.inp("bv", [1, 256])
    lq1 = st.inp("lq1", [1, 128])
    lk1 = st.inp("lk1", [1, 128])
    lq2 = st.inp("lq2", [1, 128])
    lk2 = st.inp("lk2", [1, 128])
    subw = st.inp("subw", [128, 2])
    woT = st.inp("WoT", [256, D])
    bo = st.inp("bo", [1, D])
    ident = st.inp("ident", [128, 128])
    ones_col = st.inp("ones_col", [1, 128])
    h2 = st.out("h2", [TL, D])

    ag_in = st.scratch("ag_in", [D, TL])
    ag_out = st.scratch("ag_out", [NCORES * D, TL])
    d_qT = st.scratch("d_qT", [256, T])
    d_kT = st.scratch("d_kT", [256, T])
    d_v = st.scratch("d_v", [T, 256])
    d_outp = st.scratch("d_outp", [T, D])
    d_rs = st.scratch("d_rs", [TL, D])

    cpr = max(1, TL // 512)
    def hs_chunk(c, n):
        r, tl = n // cpr, (n % cpr) * 512
        return ag_out.ap()[r * D + c * 128: r * D + (c + 1) * 128, tl: tl + 512]

    NTC_ = T // 512
    NQT = T // 128

    def band(qt):
        jlo = max(0, qt - (NBAND - 1))
        return jlo, qt - jlo + 1

    NLI = -(1.0 - LAMBDA_INIT)

    with ExitStack() as ctx:
        sb = lambda nm, s, d=F32: ctx.enter_context(nc.sbuf_tensor(nm, s, d))
        sem = lambda nm: ctx.enter_context(nc.semaphore(nm))
        hsb = [sb(f"hsb{k}", [128, 512]) for k in range(2)]
        w_q = sb("w_q", [128, NDC * 256])
        w_k = sb("w_k", [128, NDC * 256])
        w_v = sb("w_v", [128, NDC * 256])
        w_o = sb("w_o", [128, 2 * D])
        bq_sb = sb("bq_sb", [128, 2])
        bk_sb = sb("bk_sb", [128, 2])
        bv_row = sb("bv_row", [1, 256])
        bvB = sb("bvB", [128, 256])
        bo_row = sb("bo_row", [1, D])
        boB = sb("boB", [128, D])
        lrow = sb("lrow", [1, 512])
        lt = sb("lt", [1, 132])
        lam128 = sb("lam128", [128, 1])
        subw_sb = sb("subw_sb", [128, 2])
        idt = sb("idt", [128, 128])
        onc = sb("onc", [1, 128])
        onescol = sb("onescol", [128, 1])
        pev = [sb(f"pev{k}", [128, 512]) for k in range(2)]
        vev = [sb(f"vev{k}", [128, 256]) for k in range(2)]
        qsl = sb("qsl", [128, 256])
        kb = [sb(f"kb{k}", [128, NBAND * 128]) for k in range(2)]
        vb = sb("vb", [128, NBAND * 256])
        sband = sb("sband", [128, NBAND * 128])
        eband = sb("eband", [128, NBAND * 128])
        ptb = sb("ptb", [128, 128])
        asb = sb("asb", [128, 512])
        dn = sb("dn", [128, 256])
        sqv = sb("sqv", [128, 256])
        rstdB = sb("rstdB", [128, 128])
        red = sb("red", [128, 4])
        r1 = sb("r1", [1, 384])
        outp_sb = sb("outp_sb", [128, D])
        xr = sb("xr", [128, D])
        rr = sb("rr", [128, D])
        ps_all = ctx.enter_context(nc.psum_tensor("ps_all", [128, 4096], F32))
        q_sy = sem("q_sy"); q_te = sem("q_te"); q_sc = sem("q_sc")
        q_ve = sem("q_ve"); q_gp = sem("q_gp"); q_cc = sem("q_cc")
        q_gq = sem("q_gq")
        block = ctx.enter_context(nc.Block())

        pr_q = [ps_all[:, k * 512:(k + 1) * 512] for k in range(2)]
        pr_k = [ps_all[:, 1024 + k * 512: 1024 + (k + 1) * 512] for k in range(2)]
        pr_v = [ps_all[:, 2048 + k * 256: 2048 + (k + 1) * 256] for k in range(4)]
        pr_sc = ps_all[:, 0:2560]
        pr_tr = ps_all[:, 2560:2688]
        pr_ms = ps_all[0:1, 2688:2816]
        pr_a = [ps_all[:, 3072 + k * 128: 3072 + (k + 1) * 128] for k in range(4)]
        pr_o = ps_all[:, 3584:4096]

        @block.sync
        def _(sync):
            q = Q(sync, q_sy)
            q.dma(w_q[:].rearrange("p (c n) -> p c n", c=NDC),
                  wqT.rearrange("(c p) n -> p c n", p=128))
            q.dma(w_k[:].rearrange("p (c n) -> p c n", c=NDC),
                  wkT.rearrange("(c p) n -> p c n", p=128))
            q.dma(w_v[:].rearrange("p (c n) -> p c n", c=NDC),
                  wvT.rearrange("(c p) n -> p c n", p=128))
            q.dma(w_o[:].rearrange("p (h n) -> p h n", h=2),
                  woT.rearrange("(h p) n -> p h n", p=128))
            q.dma(bq_sb[:], bq[:, :])
            q.dma(bk_sb[:], bk[:, :])
            q.dma(bv_row[:], bv[:, :])
            q.dma(bo_row[:], bo[:, :])
            q.dma(lrow[:, 0:128], lq1[:, :])
            q.dma(lrow[:, 128:256], lk1[:, :])
            q.dma(lrow[:, 256:384], lq2[:, :])
            q.dma(lrow[:, 384:512], lk2[:, :])
            q.dma(subw_sb[:], subw[:, :])
            q.dma(idt[:], ident[:, :])
            q.dma(onc[:], ones_col[:, :])
            M.set("wload", q.n)
            q.wait(q_cc, M["ag_done"])
            for n in range(NTC_):
                for c in range(NDC):
                    gi = n * NDC + c
                    if gi >= 2:
                        q.wait(q_te, M[f"Bmm_{gi - 2}"])
                    q.dma(hsb[gi % 2][:], hs_chunk(c, n))
                    M.set(f"Bld_{n}_{c}", q.n)
            for qt in range(NQT):
                jlo, nj = band(qt)
                if qt == 0:
                    q.wait(q_gp, M["projst_done"])
                else:
                    q.wait(q_te, M[f"outmm_{qt - 1}_3"])
                q.dma(qsl[:].rearrange("p (s n) -> p s n", s=2),
                      d_qT.ap()[:, qt * 128:(qt + 1) * 128]
                      .rearrange("(s p) n -> p s n", p=128))
                q.dma(vb[:, 0:nj * 256].rearrange("p (j n) -> p j n", j=nj),
                      d_v.ap()[jlo * 128:(jlo + nj) * 128, :]
                      .rearrange("(j p) n -> p j n", p=128))
                M.set(f"Cq_{qt}", q.n)
                for s in range(2):
                    gi = qt * 2 + s
                    if gi >= 2:
                        q.wait(q_te, M[f"smm_{gi - 2}"])
                    q.dma(kb[gi % 2][:, 0:nj * 128],
                          d_kT.ap()[s * 128:(s + 1) * 128,
                                    jlo * 128:(jlo + nj) * 128])
                    M.set(f"kld_{qt}_{s}", q.n)
            for i in range(NTT):
                q.wait(q_cc, M["rs_done"])
                if i > 0:
                    q.wait(q_ve, M[f"resadd_{i - 1}"])
                    q.wait(q_gp, M[f"xst_{i - 1}"])
                q.dma(xr[:], x[ts(i, 128), :])
                q.dma(rr[:], d_rs.ap()[ts(i, 128), :])
                M.set(f"Dld_{i}", q.n)

        @block.tensor
        def _(tensor):
            q = Q(tensor, q_te)
            q.wait(q_sy, M["wload"])
            q.wait(q_ve, M["wq_scaled"])
            q.op(tensor.matmul(pr_tr[:, 0:128], onc[:], bv_row[:, 0:128],
                               start=True, stop=True), serial=False)
            M.set("bvmm_0", q.n)
            q.wait(q_sc, M["bvev_0"])
            q.op(tensor.matmul(pr_tr[:, 0:128], onc[:], bv_row[:, 128:256],
                               start=True, stop=True), serial=False)
            M.set("bvmm_1", q.n)
            q.wait(q_ve, M["bo_scaled"])
            for cc in range(4):
                q.wait(q_sc, M[f"boev_{cc - 1}"] if cc > 0 else M["bvev_1"])
                q.op(tensor.matmul(pr_o, onc[:],
                                   bo_row[:, cc * 512:(cc + 1) * 512],
                                   start=True, stop=True), serial=False)
                M.set(f"bomm_{cc}", q.n)
            q.wait(q_ve, M["lam_done"])
            q.wait(q_sc, M["boev_3"])
            q.op(tensor.matmul(pr_tr[:, 0:1], onc[:], lt[:, 131:132],
                               start=True, stop=True), serial=False)
            M.set("lam_mm", q.n)
            for n in range(NTC_):
                if n > 0:
                    q.wait(q_sc, M[f"qkev_{n - 1}_done"])
                    q.wait(q_ve, M[f"vev_{n - 1}_done"])
                else:
                    q.wait(q_sc, M["lam_ev"])
                for c in range(NDC):
                    gi = n * NDC + c
                    q.wait(q_sy, M[f"Bld_{n}_{c}"])
                    sp_ = (c == NDC - 1)
                    b = hsb[gi % 2][:]
                    for m in range(2):
                        q.op(tensor.matmul(pr_q[m],
                                           w_q[:, c * 256 + m * 128:
                                               c * 256 + (m + 1) * 128],
                                           b, start=(c == 0), stop=sp_),
                             serial=False)
                    for m in range(2):
                        q.op(tensor.matmul(pr_k[m],
                                           w_k[:, c * 256 + m * 128:
                                               c * 256 + (m + 1) * 128],
                                           b, start=(c == 0), stop=sp_),
                             serial=False)
                    for tt in range(4):
                        q.op(tensor.matmul(pr_v[tt], b[:, ts(tt, 128)],
                                           w_v[:, ts(c, 256)],
                                           start=(c == 0 and tt % 2 == 0),
                                           stop=sp_, skip_group_check=True),
                             serial=False)
                    M.set(f"Bmm_{gi}", q.n)
                M.set(f"Bmm_n_{n}", q.n)
            for qt in range(NQT):
                jlo, nj = band(qt)
                for s in range(2):
                    gi = qt * 2 + s
                    q.wait(q_sy, M[f"kld_{qt}_{s}"])
                    q.wait(q_sy, M[f"Cq_{qt}"])
                    if gi > 0:
                        q.wait(q_sc, M[f"sbev_{gi - 1}"])
                    else:
                        q.wait(q_sc, M[f"qkev_{NTC_ - 1}_done"])
                        q.wait(q_ve, M[f"vev_{NTC_ - 1}_done"])
                    nchunk = (nj * 128 + 511) // 512
                    for ch in range(nchunk):
                        wd = min(512, nj * 128 - ch * 512)
                        q.op(tensor.matmul(
                            pr_sc[:, ch * 512: ch * 512 + wd],
                            qsl[:, s * 128:(s + 1) * 128],
                            kb[gi % 2][:, ch * 512: ch * 512 + wd],
                            start=True, stop=True), serial=False)
                    M.set(f"smm_{gi}", q.n)
                    for j in range(nj):
                        q.wait(q_ve, M[f"pn_{gi}"])
                        if j > 0:
                            q.wait(q_sc, M[f"ptev_{gi}_{j - 1}"])
                        elif s == 1:
                            q.wait(q_sc, M[f"ptev_{gi - 1}_{nj - 1}"])
                        elif qt > 0:
                            q.wait(q_sc, M[f"rstdBev_{qt - 1}"])
                        q.op(tensor.transpose(pr_tr[:, 0:128],
                                              eband[:, j * 128:(j + 1) * 128],
                                              idt[:]), serial=False)
                        M.set(f"pT_{gi}_{j}", q.n)
                        q.wait(q_sc, M[f"ptev_{gi}_{j}"])
                        if s == 0 and j == 0:
                            q.wait(q_ve, M[f"aev_{qt - 1}"] if qt > 0 else 0)
                        for w in range(2):
                            q.op(tensor.matmul(
                                pr_a[s * 2 + w],
                                vb[:, j * 256 + w * 128: j * 256 + (w + 1) * 128],
                                ptb[:],
                                start=(s == 0 and j == 0 and w == 0),
                                stop=False, skip_group_check=True),
                                serial=False)
                        M.set(f"amm_{gi}_{j}", q.n)
                    M.set(f"amm_{gi}", q.n)
                q.wait(q_ve, M[f"dtiles_{qt}"])
                if qt > 0:
                    q.wait(q_ve, M[f"msv_{qt - 1}"])
                for k in range(2):
                    q.op(tensor.matmul(pr_ms, onescol[:],
                                       sqv[:, k * 128:(k + 1) * 128],
                                       start=(k == 0), stop=(k == 1)),
                         serial=False)
                M.set(f"msmm_{qt}", q.n)
                q.wait(q_sc, M[f"rstd_{qt}"])
                q.op(tensor.matmul(pr_tr[:, 0:128], onc[:], r1[:, 256:384],
                                   start=True, stop=True), serial=False)
                M.set(f"rbmm_{qt}", q.n)
                for oc in range(4):
                    q.wait(q_ve, M[f"dnfin_{qt}"])
                    q.wait(q_ve, M[f"outev_{qt}_{oc - 1}"] if oc > 0 else
                           (M[f"outev_{qt - 1}_3"] if qt > 0 else 0))
                    for k in range(2):
                        q.op(tensor.matmul(
                            pr_o, dn[:, k * 128:(k + 1) * 128],
                            w_o[:, k * D + oc * 512: k * D + (oc + 1) * 512],
                            start=(k == 0), stop=(k == 1)), serial=False)
                    M.set(f"outmm_{qt}_{oc}", q.n)
            M.set("te_end", q.n)

        @block.scalar
        def _(scalar):
            q = Q(scalar, q_sc)
            q.wait(q_te, M["bvmm_0"])
            q.op(scalar.copy(bvB[:, 0:128], pr_tr[:, 0:128]))
            M.set("bvev_0", q.n)
            q.wait(q_te, M["bvmm_1"])
            q.op(scalar.copy(bvB[:, 128:256], pr_tr[:, 0:128]))
            M.set("bvev_1", q.n)
            for cc in range(4):
                q.wait(q_te, M[f"bomm_{cc}"])
                q.op(scalar.copy(boB[:, cc * 512:(cc + 1) * 512], pr_o))
                M.set(f"boev_{cc}", q.n)
            q.wait(q_ve, M["lamv"])
            q.op(scalar.activation(lt[:, 128:129], lt[:, 128:129], AF.Exp))
            q.op(scalar.activation(lt[:, 129:130], lt[:, 129:130], AF.Exp))
            M.set("lamx", q.n)
            q.wait(q_te, M["lam_mm"])
            q.op(scalar.copy(lam128[:], pr_tr[:, 0:1]))
            M.set("lam_ev", q.n)
            for n in range(NTC_):
                q.wait(q_te, M[f"Bmm_n_{n}"])
                for u in range(4):
                    gi = n * 4 + u
                    if gi >= 2:
                        q.wait(q_gp, M[f"qkst_{(gi - 2) // 4}_{(gi - 2) % 4}"])
                    m = u % 2
                    prt = pr_q[m] if u < 2 else pr_k[m]
                    bias = bq_sb if u < 2 else bk_sb
                    q.op(scalar.activation(pev[gi % 2][:], prt, AF.Identity,
                                           bias=bias[:, m:m + 1]))
                    M.set(f"qkev_{n}_{u}", q.n)
                M.set(f"qkev_{n}_done", q.n)
            for qt in range(NQT):
                jlo, nj = band(qt)
                for s in range(2):
                    gi = qt * 2 + s
                    q.wait(q_te, M[f"smm_{gi}"])
                    nchunk = (nj * 128 + 511) // 512
                    for ch in range(nchunk):
                        wd = min(512, nj * 128 - ch * 512)
                        q.op(scalar.copy(sband[:, ch * 512: ch * 512 + wd],
                                         pr_sc[:, ch * 512: ch * 512 + wd]))
                    M.set(f"sbev_{gi}", q.n)
                    q.wait(q_gq, M[f"sel_{gi}"])
                    q.wait(q_ve, M[f"nm_{gi}"])
                    if gi > 0:
                        q.wait(q_te, M[f"amm_{gi - 1}"])
                    q.op(scalar.activation(eband[:, 0:nj * 128],
                                           sband[:, 0:nj * 128], AF.Exp,
                                           bias=red[:, 0:1]))
                    M.set(f"exp_{gi}", q.n)
                    for j in range(nj):
                        q.wait(q_te, M[f"pT_{gi}_{j}"])
                        if j > 0:
                            q.wait(q_te, M[f"amm_{gi}_{j - 1}"])
                        q.op(scalar.copy(ptb[:], pr_tr[:, 0:128]))
                        M.set(f"ptev_{gi}_{j}", q.n)
                q.wait(q_ve, M[f"msv_{qt}"])
                q.op(scalar.activation(r1[:, 128:256], r1[:, 0:128], AF.Ln))
                q.op(scalar.activation(r1[:, 256:384], r1[:, 128:256], AF.Exp,
                                       scale=-0.5))
                M.set(f"rstd_{qt}", q.n)
                q.wait(q_te, M[f"rbmm_{qt}"])
                q.op(scalar.copy(rstdB[:], pr_tr[:, 0:128]))
                M.set(f"rstdBev_{qt}", q.n)
            M.set("sc_end", q.n)

        @block.vector
        def _(vector):
            q = Q(vector, q_ve)
            q.wait(q_sy, M["wload"])
            q.op(vector.tensor_scalar(w_q[:], w_q[:], ATT_SCALE, None,
                                      ALU.mult))
            q.op(vector.tensor_scalar(bq_sb[:], bq_sb[:], ATT_SCALE, None,
                                      ALU.mult))
            M.set("wq_scaled", q.n)
            q.op(vector.tensor_scalar(bo_row[:], bo_row[:], 1.0 / NCORES,
                                      None, ALU.mult))
            M.set("bo_scaled", q.n)
            q.op(vector.memset(onescol[:], 1.0))
            M.set("onescol_done", q.n)
            q.op(vector.tensor_tensor(lt[:, 0:128], lrow[:, 0:128],
                                      lrow[:, 128:256], ALU.mult))
            q.op(vector.reduce_sum(lt[:, 128:129], lt[:, 0:128], axis=AX.X))
            q.op(vector.tensor_tensor(lt[:, 0:128], lrow[:, 256:384],
                                      lrow[:, 384:512], ALU.mult))
            q.op(vector.reduce_sum(lt[:, 129:130], lt[:, 0:128], axis=AX.X))
            M.set("lamv", q.n)
            q.wait(q_sc, M["lamx"])
            q.op(vector.tensor_tensor(lt[:, 130:131], lt[:, 128:129],
                                      lt[:, 129:130], ALU.subtract))
            q.op(vector.tensor_scalar(lt[:, 131:132], lt[:, 130:131],
                                      LAMBDA_INIT, None, ALU.add))
            M.set("lam_done", q.n)
            for n in range(NTC_):
                q.wait(q_te, M[f"Bmm_n_{n}"])
                for tt in range(4):
                    gi = n * 4 + tt
                    if gi >= 2:
                        q.wait(q_gp, M[f"vst_{(gi - 2) // 4}_{(gi - 2) % 4}"])
                    q.op(vector.tensor_tensor(vev[gi % 2][:], pr_v[tt],
                                              bvB[:], ALU.add))
                    M.set(f"vev_{n}_{tt}", q.n)
                M.set(f"vev_{n}_done", q.n)
            for qt in range(NQT):
                jlo, nj = band(qt)
                for s in range(2):
                    gi = qt * 2 + s
                    q.wait(q_gq, M[f"sel_{gi}"])
                    q.op(vector.reduce_max(red[:, 0:1], sband[:, 0:nj * 128],
                                           axis=AX.X, negate=True))
                    M.set(f"nm_{gi}", q.n)
                    q.wait(q_sc, M[f"exp_{gi}"])
                    q.op(vector.reduce_sum(red[:, 1:2], eband[:, 0:nj * 128],
                                           axis=AX.X))
                    q.op(vector.reciprocal(red[:, 2:3], red[:, 1:2]))
                    q.op(vector.tensor_scalar(eband[:, 0:nj * 128],
                                              eband[:, 0:nj * 128],
                                              red[:, 2:3], None, ALU.mult))
                    M.set(f"pn_{gi}", q.n)
                q.wait(q_te, M[f"amm_{qt * 2 + 1}"])
                q.op(vector.tensor_copy(asb[:], ps_all[:, 3072:3584]))
                M.set(f"aev_{qt}", q.n)
                q.op(vector.scalar_tensor_tensor(dn[:, 0:128],
                                                 asb[:, 256:384], lam128[:],
                                                 asb[:, 0:128], ALU.mult,
                                                 ALU.subtract))
                q.op(vector.scalar_tensor_tensor(dn[:, 128:256],
                                                 asb[:, 384:512], lam128[:],
                                                 asb[:, 128:256], ALU.mult,
                                                 ALU.subtract))
                q.op(vector.tensor_tensor(sqv[:], dn[:], dn[:], ALU.mult))
                M.set(f"dtiles_{qt}", q.n)
                q.wait(q_te, M[f"msmm_{qt}"])
                q.op(vector.tensor_scalar(r1[:, 0:128], pr_ms, 1.0 / 256.0,
                                          EPS, ALU.mult, ALU.add))
                M.set(f"msv_{qt}", q.n)
                q.wait(q_sc, M[f"rstdBev_{qt}"])
                q.op(vector.tensor_tensor(dn[:, 0:128], dn[:, 0:128],
                                          rstdB[:], ALU.mult))
                q.op(vector.tensor_tensor(dn[:, 128:256], dn[:, 128:256],
                                          rstdB[:], ALU.mult))
                for k in range(2):
                    q.op(vector.tensor_scalar(dn[:, k * 128:(k + 1) * 128],
                                              dn[:, k * 128:(k + 1) * 128],
                                              subw_sb[:, k:k + 1], NLI,
                                              ALU.mult, ALU.mult))
                M.set(f"dnfin_{qt}", q.n)
                for oc in range(4):
                    q.wait(q_te, M[f"outmm_{qt}_{oc}"])
                    if oc == 0 and qt > 0:
                        q.wait(q_gp, M[f"outst_{qt - 1}"])
                    q.op(vector.tensor_tensor(outp_sb[:, oc * 512:
                                                      (oc + 1) * 512],
                                              pr_o, boB[:, oc * 512:
                                                        (oc + 1) * 512],
                                              ALU.add))
                    M.set(f"outev_{qt}_{oc}", q.n)
            for i in range(NTT):
                q.wait(q_sy, M[f"Dld_{i}"])
                q.op(vector.tensor_add(xr[:], xr[:], rr[:]))
                M.set(f"resadd_{i}", q.n)
            M.set("ve_end", q.n)

        @block.gpsimd
        def _(gpsimd):
            q = Q(gpsimd, q_gp)
            qc = Q(gpsimd, q_cc)
            qs = Q(gpsimd, q_gq)
            fill_reg = gpsimd.to_reg(-1e30)
            q.dma(ag_in.ap(), xnT[:, :])
            gpsimd.wait_ge(q_gp, q.n)
            qc.op(gpsimd.collective_compute(
                "AllGather", ALU.bypass, replica_groups=[list(range(NCORES))],
                ins=[ag_in.ap().opt()], outs=[ag_out.ap().opt()]))
            M.set("ag_done", qc.n)
            for n in range(NTC_):
                for u in range(4):
                    gi = n * 4 + u
                    q.wait(q_sc, M[f"qkev_{n}_{u}"])
                    dst = d_qT if u < 2 else d_kT
                    m = u % 2
                    q.dma(dst.ap()[m * 128:(m + 1) * 128,
                                   n * 512:(n + 1) * 512], pev[gi % 2][:])
                    M.set(f"qkst_{n}_{u}", q.n)
                for tt in range(4):
                    gi = n * 4 + tt
                    q.wait(q_ve, M[f"vev_{n}_{tt}"])
                    q.dma(d_v.ap()[n * 512 + tt * 128: n * 512 + (tt + 1) * 128,
                                   0:256], vev[gi % 2][:])
                    M.set(f"vst_{n}_{tt}", q.n)
            M.set("projst_done", q.n)
            for qt in range(NQT):
                jlo, nj = band(qt)
                for s in range(2):
                    gi = qt * 2 + s
                    q.wait(q_sc, M[f"sbev_{gi}"])
                    base1 = (qt - jlo) * 128
                    qs.op(gpsimd.affine_select(
                        sband[:, 0:nj * 128], sband[:, 0:nj * 128],
                        pattern=[[-1, nj * 128]], compare_op=ALU.is_ge,
                        fill=fill_reg, base=base1, channel_multiplier=1))
                    if base1 + 127 >= WINDOW:
                        qs.op(gpsimd.affine_select(
                            sband[:, 0:nj * 128], sband[:, 0:nj * 128],
                            pattern=[[1, nj * 128]], compare_op=ALU.is_ge,
                            fill=fill_reg, base=WINDOW - 1 - base1,
                            channel_multiplier=-1))
                    M.set(f"sel_{gi}", qs.n)
                q.wait(q_ve, M[f"outev_{qt}_3"])
                q.dma(d_outp.ap()[qt * 128:(qt + 1) * 128, :], outp_sb[:])
                M.set(f"outst_{qt}", q.n)
            gpsimd.wait_ge(q_gp, q.n)
            qc.op(gpsimd.collective_compute(
                "ReduceScatter", ALU.add, replica_groups=[list(range(NCORES))],
                ins=[d_outp.ap().opt()], outs=[d_rs.ap().opt()]))
            M.set("rs_done", qc.n)
            for i in range(NTT):
                q.wait(q_ve, M[f"resadd_{i}"])
                q.dma(h2[ts(i, 128), :], xr[:])
                M.set(f"xst_{i}", q.n)
            gpsimd.wait_ge(q_gp, q.n)
    return st


def build_diffattn(st: Stage):
    return twopass(emit_diffattn, st)


def run_diffattn(x_slices, xnT_slices, wqkv_w, wqkv_b, out_proj_w, out_proj_b,
                 lambda_q1, lambda_k1, lambda_q2, lambda_k2, subln_w):
    st = Stage("diffattn")
    build_diffattn(st)
    in_maps = []
    for hp in range(NCORES):
        k1h = 0 if hp < 4 else 2
        k2h = k1h + 1
        qrows = slice(2 * hp * 128, (2 * hp + 2) * 128)
        krows = [2048 + k1h * 128, 2048 + k2h * 128]
        vrows = [2048 + 512 + k1h * 128, 2048 + 512 + k2h * 128]
        wk = np.concatenate([wqkv_w[r: r + 128] for r in krows], 0)
        wv = np.concatenate([wqkv_w[r: r + 128] for r in vrows], 0)
        bq_ = wqkv_b[qrows].reshape(2, 128).T
        bk_ = np.stack([wqkv_b[r: r + 128] for r in krows], 1)
        bv_ = np.concatenate([wqkv_b[r: r + 128] for r in vrows], 0)
        in_maps.append({
            "x": np.ascontiguousarray(x_slices[hp]),
            "xnT": np.ascontiguousarray(xnT_slices[hp]),
            "WqT": np.ascontiguousarray(wqkv_w[qrows].T),
            "WkT": np.ascontiguousarray(wk.T),
            "WvT": np.ascontiguousarray(wv.T),
            "bq": np.ascontiguousarray(bq_),
            "bk": np.ascontiguousarray(bk_),
            "bv": np.ascontiguousarray(bv_.reshape(1, 256)),
            "lq1": np.ascontiguousarray(lambda_q1.reshape(1, 128)),
            "lk1": np.ascontiguousarray(lambda_k1.reshape(1, 128)),
            "lq2": np.ascontiguousarray(lambda_q2.reshape(1, 128)),
            "lk2": np.ascontiguousarray(lambda_k2.reshape(1, 128)),
            "subw": np.ascontiguousarray(subln_w.reshape(2, 128).T),
            "WoT": np.ascontiguousarray(
                out_proj_w[:, hp * 256:(hp + 1) * 256].T),
            "bo": np.ascontiguousarray(out_proj_b.reshape(1, D)),
            "ident": _IDENT, "ones_col": _ONES_COL,
        })
    res = st.run(in_maps)
    return [res[i]["h2"] for i in range(NCORES)]



def emit_mlp(st: Stage, M: Milestones):
    nc = st.nc
    x = st.inp("x", [TL, D])
    xnT = st.inp("xnT", [D, TL])
    wgT = st.inp("WgT", [D, INTER])
    wuT = st.inp("WuT", [D, INTER])
    wdT = st.inp("WdT", [INTER, D])
    x_new = st.out("x_new", [TL, D])

    NTCH = TL // 512

    with ExitStack() as ctx:
        sb = lambda nm, s, d=F32: ctx.enter_context(nc.sbuf_tensor(nm, s, d))
        sem = lambda nm: ctx.enter_context(nc.semaphore(nm))
        mt = sb("mt", [128, NMT * 512])
        xh = sb("xh", [128, NDC * 512])
        xr4 = xh
        wst = [sb(f"wst{k}", [128, NDC * 128]) for k in range(2)]
        wdn = [sb(f"wdn{k}", [128, 512]) for k in range(2)]
        upev = sb("upev", [128, 512])
        gtev = sb("gtev", [128, 512])
        silv = sb("silv", [128, 512])
        ps_all = ctx.enter_context(nc.psum_tensor("ps_all", [128, 4096], F32))
        q_sy = sem("q_sy"); q_te = sem("q_te"); q_sc = sem("q_sc")
        q_ve = sem("q_ve"); q_gp = sem("q_gp")
        block = ctx.enter_context(nc.Block())

        pr_up = ps_all[:, 1024:1536]
        pr_gt = ps_all[:, 1536:2048]
        pr_dn = [ps_all[:, 2048 + k * 512: 2048 + (k + 1) * 512] for k in range(4)]

        def mtile(mi):
            return mt[:, mi * 512:(mi + 1) * 512]

        @block.sync
        def _(sync):
            q = Q(sync, q_sy)
            for tc in range(NTCH):
                if tc > 0:
                    q.wait(q_te, M[f"upgate_mm_done_{tc - 1}"])
                    q.wait(q_gp, M[f"xst_{tc - 1}_3"])
                q.dma(xh[:].rearrange("p (c t) -> p c t", c=NDC),
                      xnT[:, tc * 512:(tc + 1) * 512]
                      .rearrange("(c p) t -> p c t", p=128))
                M.set(f"xld_{tc}", q.n)
                for m in range(NMT):
                    for wi, wt in enumerate((wuT, wgT)):
                        gi = (tc * NMT + m) * 2 + wi
                        if gi >= 2:
                            q.wait(q_te, M[f"ug_mm_{gi - 2}"])
                        q.dma(wst[wi][:].rearrange("p (c n) -> p c n", c=NDC),
                              wt[:, m * 128:(m + 1) * 128]
                              .rearrange("(c p) n -> p c n", p=128))
                        M.set(f"wld_{tc}_{m}_{wi}", q.n)
                for v in range(4):
                    q.wait(q_te, M[f"upgate_mm_done_{tc}"])
                    q.dma(xr4[:, v * D:(v + 1) * D],
                          x[tc * 512 + v * 128: tc * 512 + (v + 1) * 128, :])
                    M.set(f"xrld_{tc}_{v}", q.n)
                for o in range(4):
                    for k in range(NMT):
                        gi = o * NMT + k
                        if gi >= 2:
                            q.wait(q_te, M[f"dn_mm_{tc}_{gi - 2}"])
                        q.dma(wdn[gi % 2][:],
                              wdT[k * 128:(k + 1) * 128,
                                  o * 512:(o + 1) * 512])
                        M.set(f"dld_{tc}_{o}_{k}", q.n)

        @block.tensor
        def _(tensor):
            q = Q(tensor, q_te)
            for tc in range(NTCH):
                q.wait(q_sy, M[f"xld_{tc}"])
                for m in range(NMT):
                    for wi in range(2):
                        gi = (tc * NMT + m) * 2 + wi
                        q.wait(q_sy, M[f"wld_{tc}_{m}_{wi}"])
                        if gi >= 2:
                            q.wait(q_sc, M[f"ugev_{gi - 2}"])
                        prd = pr_up if wi == 0 else pr_gt
                        for c in range(NDC):
                            q.op(tensor.matmul(prd, wst[wi][:, ts(c, 128)],
                                               xh[:, ts(c, 512)],
                                               start=(c == 0),
                                               stop=(c == NDC - 1)), serial=False)
                        M.set(f"ug_mm_{gi}", q.n)
                M.set(f"upgate_mm_done_{tc}", q.n)
                for o in range(4):
                    for k in range(NMT):
                        gi = o * NMT + k
                        q.wait(q_sy, M[f"dld_{tc}_{o}_{k}"])
                        if k == 0:
                            q.wait(q_ve, M[f"mt_{tc}_{NMT - 1}"])
                            if o > 0:
                                q.wait(q_ve, M[f"ydn_{tc}_{o - 1}"])
                            elif tc > 0:
                                q.wait(q_ve, M[f"ydn_{tc - 1}_3"])
                        for v in range(4):
                            q.op(tensor.matmul(
                                pr_dn[v], mtile(k)[:, v * 128:(v + 1) * 128],
                                wdn[gi % 2][:], start=(k == 0),
                                stop=(k == NMT - 1)), serial=False)
                        M.set(f"dn_mm_{tc}_{gi}", q.n)
                    M.set(f"dnmm_{tc}_{o}", q.n)

        @block.scalar
        def _(scalar):
            q = Q(scalar, q_sc)
            for tc in range(NTCH):
                for m in range(NMT):
                    gi0 = (tc * NMT + m) * 2
                    q.wait(q_te, M[f"ug_mm_{gi0}"])
                    q.op(scalar.copy(upev[:], pr_up))
                    M.set(f"ugev_{gi0}", q.n)
                    q.wait(q_te, M[f"ug_mm_{gi0 + 1}"])
                    q.op(scalar.copy(gtev[:], pr_gt))
                    M.set(f"ugev_{gi0 + 1}", q.n)
                    q.op(scalar.activation(silv[:], gtev[:], AF.Silu))
                    M.set(f"sil_{tc}_{m}", q.n)
                    q.wait(q_ve, M[f"mt_{tc}_{m}"])

        @block.vector
        def _(vector):
            q = Q(vector, q_ve)
            for tc in range(NTCH):
                for m in range(NMT):
                    q.wait(q_sc, M[f"sil_{tc}_{m}"])
                    q.op(vector.tensor_tensor(mtile(m), upev[:], silv[:],
                                              ALU.mult))
                    M.set(f"mt_{tc}_{m}", q.n)
                for o in range(4):
                    q.wait(q_te, M[f"dnmm_{tc}_{o}"])
                    for v in range(4):
                        if o == 0:
                            q.wait(q_sy, M[f"xrld_{tc}_{v}"])
                        xrv = xr4[:, v * D + o * 512: v * D + (o + 1) * 512]
                        q.op(vector.tensor_tensor(xrv, pr_dn[v], xrv, ALU.add))
                    M.set(f"ydn_{tc}_{o}", q.n)
                for v in range(4):
                    M.set(f"res_{tc}_{v}", q.n)

        @block.gpsimd
        def _(gpsimd):
            q = Q(gpsimd, q_gp)
            for tc in range(NTCH):
                for v in range(4):
                    q.wait(q_ve, M[f"res_{tc}_{v}"])
                    q.dma(x_new[tc * 512 + v * 128: tc * 512 + (v + 1) * 128, :],
                          xr4[:, v * D:(v + 1) * D])
                    M.set(f"xst_{tc}_{v}", q.n)
            gpsimd.wait_ge(q_gp, q.n)
    return st


def build_mlp(st: Stage):
    return twopass(emit_mlp, st)


def run_mlp(x_slices, xnT_slices, fc1_w, fc2_w):
    st = Stage("mlp")
    build_mlp(st)
    base = {
        "WgT": np.ascontiguousarray(fc1_w[:INTER].T),
        "WuT": np.ascontiguousarray(fc1_w[INTER:].T),
        "WdT": np.ascontiguousarray(fc2_w.T),
    }
    in_maps = []
    for i in range(NCORES):
        m = dict(base)
        m["x"] = np.ascontiguousarray(x_slices[i])
        m["xnT"] = np.ascontiguousarray(xnT_slices[i])
        in_maps.append(m)
    res = st.run(in_maps)
    return [res[i]["x_new"] for i in range(NCORES)]



def build_add(st: Stage):
    nc = st.nc
    a = st.inp("a", [TL, D])
    b = st.inp("b", [TL, D])
    o = st.out("o", [TL, D])
    with ExitStack() as ctx:
        sb = lambda nm, s, d=F32: ctx.enter_context(nc.sbuf_tensor(nm, s, d))
        sem = lambda nm: ctx.enter_context(nc.semaphore(nm))
        at = sb("at", [128, D]); bt = sb("bt", [128, D])
        q_sy = sem("q_sy"); q_ve = sem("q_ve"); q_gp = sem("q_gp")
        block = ctx.enter_context(nc.Block())

        @block.sync
        def _(sync):
            q = Q(sync, q_sy)
            for i in range(NTT):
                if i > 0:
                    q.wait(q_gp, 16 * i)
                q.dma(at[:], a[ts(i, 128), :])
                q.dma(bt[:], b[ts(i, 128), :])

        @block.vector
        def _(vector):
            q = Q(vector, q_ve)
            for i in range(NTT):
                q.wait(q_sy, 32 * (i + 1))
                q.op(vector.tensor_add(at[:], at[:], bt[:]))

        @block.gpsimd
        def _(gpsimd):
            q = Q(gpsimd, q_gp)
            for i in range(NTT):
                q.wait(q_ve, i + 1)
                q.dma(o[ts(i, 128), :], at[:])
            gpsimd.wait_ge(q_gp, q.n)
    return st


def run_add(a_slices, b_slices):
    st = Stage("add")
    build_add(st)
    in_maps = [{"a": np.ascontiguousarray(a_slices[i]),
                "b": np.ascontiguousarray(b_slices[i])} for i in range(NCORES)]
    res = st.run(in_maps)
    return [res[i]["o"] for i in range(NCORES)]



def kernel(**inputs):
    g = lambda k: np.ascontiguousarray(np.asarray(inputs[k], np.float32))
    x = g("L0_attn_hidden_states").reshape(T, D)
    xs = [np.ascontiguousarray(x[i * TL:(i + 1) * TL]) for i in range(NCORES)]

    xn, xnT = run_norm(xs, mode="ln")
    xs = run_l0ffn(xs, xnT, g("L0_ffn_A"), g("L0_ffn_Bw"), g("L0_ffn_Wg"))
    xn, xnT = run_norm(xs, mode="ln")
    xs = run_l0attn(xs, xnT, g("L0_attn_Wq"), g("L0_attn_Wk"), g("L0_attn_Wv"),
                    g("L0_attn_Wg"), g("L0_attn_Wo"), g("L0_attn_Ws1"),
                    g("L0_attn_Ws2"), g("L0_attn_bs2"),
                    g("L0_attn_g_norm_weight"))
    xn, xnT = run_norm(xs, mode="ln")
    xs = run_l0ffn(xs, xnT, g("L0_ffn_A"), g("L0_ffn_Bw"), g("L0_ffn_Wg"))

    xn, xnT = run_norm(xs, mode="ln")
    xs = run_l1ffn(xs, xnT, g("L1_ffn_SU_ug"), g("L1_ffn_SU_dp"),
                   g("L1_ffn_Wscale_ug"), g("L1_ffn_Wscale_dp"),
                   g("L1_ffn_W_up"), g("L1_ffn_W_gate"), g("L1_ffn_W_down"))
    z0, _ = run_norm(xs, mode="ln")
    _, zrT = run_norm(z0, mode="rms", w=g("L1_attn_mamba_norm_w"))
    h = run_mamba(z0, zrT, g("L1_attn_in_proj_w"), g("L1_attn_conv_w"),
                  g("L1_attn_conv_b"), g("L1_attn_x_proj_w"),
                  g("L1_attn_dt_proj_w"), g("L1_attn_dt_proj_b"),
                  g("L1_attn_A_log"), g("L1_attn_D_param"),
                  g("L1_attn_mamba_out_w"))
    _, hnT = run_norm(h, mode="rms", w=g("L1_attn_attn_norm_w"))
    h = run_diffattn(h, hnT, g("L1_attn_wqkv_w"), g("L1_attn_wqkv_b"),
                     g("L1_attn_out_proj_w"), g("L1_attn_out_proj_b"),
                     g("L1_attn_lambda_q1"), g("L1_attn_lambda_k1"),
                     g("L1_attn_lambda_q2"), g("L1_attn_lambda_k2"),
                     g("L1_attn_subln_w"))
    _, mnT = run_norm(h, mode="rms", w=g("L1_attn_mlp_norm_w"))
    h = run_mlp(h, mnT, g("L1_attn_fc1_w"), g("L1_attn_fc2_w"))
    xs = run_add(xs, h)

    xn, xnT = run_norm(xs, mode="ln")
    xs = run_l1ffn(xs, xnT, g("L1_ffn_SU_ug"), g("L1_ffn_SU_dp"),
                   g("L1_ffn_Wscale_ug"), g("L1_ffn_Wscale_dp"),
                   g("L1_ffn_W_up"), g("L1_ffn_W_gate"), g("L1_ffn_W_down"))

    out = np.concatenate(xs, 0).reshape(1, T, D).astype(np.float32)
    return out
